# version 38
# baseline (speedup 1.0000x reference)
import numpy as np
import ml_dtypes
import concourse.bass as bass
import concourse.mybir as mybir
from concourse.bass_utils import run_bass_kernel_spmd

F32 = mybir.dt.float32
BF16 = mybir.dt.bfloat16
I32 = mybir.dt.int32
AF = mybir.ActivationFunctionType
ALU = mybir.AluOpType
AX = mybir.AxisListType

S = 2048
D = 1024
NH = 16
FF = 3072
NJ = S // 128
ND = D // 128
NHC = FF // 128
EPS = 1e-5
QSCALE = 0.125
FFS = 1.0 / (127.0 * 127.0)
LNEG = -1.0e9


class Prog:

    ENGINES = ("pe", "act", "dve", "pool", "sp")

    def __init__(self, nc):
        self.nc = nc
        self.ops = []
        self.res = {}
        self.sems = {name: nc.alloc_semaphore("s_" + name)
                     for name in ("pe", "act", "dve", "pool", "qsp", "qact",
                                  "qpool", "qx", "qx2", "qx3", "qx4",
                                  "qsp2", "qsp3")}

    def op(self, engine, emit, reads=(), writes=(), queue=None, war=(),
           deps=(), bump=()):
        idx = len(self.ops)
        dep = set(deps)
        for r in reads:
            e = self.res.get(r)
            if e is not None and e[0] is not None:
                dep.add(e[0])
        for w in tuple(writes) + tuple(war):
            e = self.res.get(w)
            if e is not None:
                if e[0] is not None:
                    dep.add(e[0])
                dep |= e[1]
        dep.discard(idx)
        self.ops.append({"engine": engine, "emit": emit, "deps": dep,
                         "queue": queue, "token": None})
        for r in reads:
            self.res.setdefault(r, [None, set()])[1].add(idx)
        for w in writes:
            self.res[w] = [idx, set()]
        for w in bump:
            self.res.setdefault(w, [None, set()])[0] = idx
        return idx

    def emit_all(self, final_deps):
        ops = self.ops
        needed = set(final_deps)
        for o in ops:
            needed |= o["deps"]
        cnt = {k: 0 for k in self.sems}
        for i, o in enumerate(ops):
            ch = o["queue"] if o["queue"] else o["engine"]
            if o["queue"] is not None or i in needed:
                inc = 16 if o["queue"] else 1
                cnt[ch] += inc
                o["token"] = (ch, cnt[ch], inc)
        per_eng = {e: [] for e in self.ENGINES}
        for i, o in enumerate(ops):
            per_eng[o["engine"]].append(i)
        sems = self.sems
        burst_end_val = {}
        burst_settle = {}
        for i, o in enumerate(ops):
            if o["queue"] is not None:
                _, val, _ = o["token"]
                burst_end_val[i] = val
                burst_settle[i] = val - 16

        def body(ename, eng):
            wm = {}

            def wait(ch, val):
                if wm.get(ch, 0) < val:
                    eng.wait_ge(sems[ch], val)
                    wm[ch] = val

            for i in per_eng[ename]:
                o = ops[i]
                if o["queue"] is not None and i in burst_settle:
                    wait(o["queue"], burst_settle[i])
                for d in sorted(o["deps"]):
                    ch, val, _ = ops[d]["token"]
                    if ops[d]["queue"] is not None:
                        val = burst_end_val[d]
                    wait(ch, val)
                inst = o["emit"](eng)
                if o["token"] is not None:
                    ch, val, inc = o["token"]
                    inst.then_inc(sems[ch], inc)
            if ename == "sp":
                for d in sorted(final_deps):
                    ch, val, _ = ops[d]["token"]
                    wait(ch, val)

        with self.nc.Block() as blk:
            blk.tensor(lambda e: body("pe", e))
            blk.scalar(lambda e: body("act", e))
            blk.vector(lambda e: body("dve", e))
            blk.gpsimd(lambda e: body("pool", e))
            blk.sync(lambda e: body("sp", e))


def build_program(trunc=0, trunc_step=None):
    nc = bass.Bass("TRN2", target_bir_lowering=False)

    dX = nc.dram_tensor("x", [S, D], F32, kind="ExternalInput").ap()
    dWq = nc.dram_tensor("wq", [D, D], F32, kind="ExternalInput").ap()
    dWk = nc.dram_tensor("wk", [D, D], F32, kind="ExternalInput").ap()
    dWv = nc.dram_tensor("wv", [D, D], F32, kind="ExternalInput").ap()
    dWo = nc.dram_tensor("wo", [D, D], F32, kind="ExternalInput").ap()
    dBq = nc.dram_tensor("bqr", [128, ND], F32, kind="ExternalInput").ap()
    dBk = nc.dram_tensor("bkr", [128, ND], F32, kind="ExternalInput").ap()
    dBv = nc.dram_tensor("bvr", [128, ND], F32, kind="ExternalInput").ap()
    dBo = nc.dram_tensor("bor", [128, ND], F32, kind="ExternalInput").ap()
    dWg = nc.dram_tensor("wg", [D, FF], I32, kind="ExternalInput").ap()
    dWu = nc.dram_tensor("wu", [D, FF], I32, kind="ExternalInput").ap()
    dWd = nc.dram_tensor("wd", [FF, D], I32, kind="ExternalInput").ap()
    dCst = nc.dram_tensor("consts", [128, 448], BF16, kind="ExternalInput").ap()
    dY = nc.dram_tensor("y", [S, D], F32, kind="ExternalOutput").ap()

    s_x = nc.alloc_sbuf_tensor("s_x", [128, NJ * D], BF16)
    s_f = nc.alloc_sbuf_tensor("s_f", [128, ND * S], BF16)
    s_b = nc.alloc_sbuf_tensor("s_b", [128, 49536], BF16)
    s_at = nc.alloc_sbuf_tensor("s_at", [128, 4096], BF16)
    s_ln = nc.alloc_sbuf_tensor("s_ln", [128, 2048], BF16)
    s_dmp = nc.alloc_sbuf_tensor("s_dmp", [128, 2048], BF16)
    s_y2 = nc.alloc_sbuf_tensor("s_y2", [128, 2048], F32)
    s_wb = nc.alloc_sbuf_tensor("s_wb", [128, 4096], BF16)
    s_wd = nc.alloc_sbuf_tensor("s_wd", [128, 6144], BF16)
    s_cst = nc.alloc_sbuf_tensor("s_cst", [128, 448], BF16)
    s_bq = nc.alloc_sbuf_tensor("s_bq", [128, ND], F32)
    s_bk = nc.alloc_sbuf_tensor("s_bk", [128, ND], F32)
    s_bv = nc.alloc_sbuf_tensor("s_bv", [128, ND], F32)
    s_bo = nc.alloc_sbuf_tensor("s_bo", [128, ND], F32)
    s_Bw = nc.alloc_sbuf_tensor("s_Bw", [1, 128], F32)
    s_nB = nc.alloc_sbuf_tensor("s_nB", [128, ND], F32)
    s_st = nc.alloc_sbuf_tensor("s_st", [128, 128], F32)
    s_sx = nc.alloc_sbuf_tensor("s_sx", [128, NJ], F32)

    pS = nc.alloc_psum_tensor("pS", [128, 2048], F32)
    pO = nc.alloc_psum_tensor("pO", [128, 2048], F32)

    P = Prog(nc)
    op = P.op

    IDT = s_cst[:, 0:128]
    LMSK = s_cst[:, 128:256]
    NEGI = s_cst[:, 256:384]
    ONES = s_cst[:, 384:448]

    Q0, K0, V0 = 0, ND * S, 2 * ND * S

    def q_fm(c):
        return s_b[:, Q0 + S * c: Q0 + S * (c + 1)]

    def k_fm(c):
        return s_b[:, K0 + S * c: K0 + S * (c + 1)]

    def xfm(c):
        return s_f[:, S * c: S * (c + 1)]

    def ofm(c):
        return s_b[:, Q0 + S * c: Q0 + S * (c + 1)]

    def acc_view(i):
        return pS[:, 512 * i:512 * (i + 1)]

    def psb(tensor, lo, hi):
        base = 0 if tensor is pS else 4
        return ["ps%d" % b for b in
                range(base + lo // 512, base + (hi - 1) // 512 + 1)]

    ALLXNT = ["xnT%d" % j for j in range(NJ)]
    ALLYNT = ["ynT%d" % j for j in range(NJ)]
    ALLOH = ["o_h%d_%d" % (h, b) for h in range(NH) for b in range(2)]

    XQ = ("qx", "qx2", "qx3", "qx4")

    def x_chunk_dma(j):
        src = dX[128 * j:128 * (j + 1), :]
        dst = s_x[:, 1024 * j:1024 * (j + 1)]
        op("pool", lambda e, src=src, dst=dst: e.dma_start(dst, src),
           writes=["x%d" % j], queue=XQ[j % 4])

    def x_dma(t):
        x_chunk_dma(2 * t)
        x_chunk_dma(2 * t + 1)

    pre_dma = [None]

    def w_chunk_dma_pre():
        src0 = dWq.rearrange("(kc p) n -> p kc n", p=128)[:, :, 0:128]
        dstv = s_wb[:, 0:1024].rearrange("p (kc n) -> p kc n", n=128)
        op("pool", lambda e, src0=src0, dstv=dstv: e.dma_start(dstv, src0),
           writes=["wb0"], queue="qpool")

    w_chunk_dma_pre()
    x_dma(0)
    x_dma(1)
    op("act", lambda e: e.dma_start(s_cst[:], dCst), writes=["cst"], queue="qact")
    op("act", lambda e: e.dma_start(s_bq[:], dBq), writes=["bq"], queue="qact")
    op("act", lambda e: e.dma_start(s_bk[:], dBk), writes=["bk"], queue="qact")
    op("act", lambda e: e.dma_start(s_bv[:], dBv), writes=["bv"], queue="qact")
    op("act", lambda e: e.dma_start(s_bo[:], dBo), writes=["bo"], queue="qact")

    def emit_ln(src_ap, src_res, dst_ap, dst_res, pid=0, eng="dve",
                dst_war=(), war0=(), m_cell=None, m_res=None, m_out=None):
        b = 16 * pid

        def cell(i):
            return s_st[:, b + i:b + i + 1]

        r = lambda n: "st%d_%s" % (pid, n)
        dmp = s_dmp[:, 0:1024]
        if m_cell is None:
            if m_out is not None:
                mc, mres = m_out
            else:
                mc, mres = cell(2), r("m")
            op("dve", lambda e, mc=mc: e.tensor_scalar(
                dmp, src_ap, 1.0 / D, 0.0, ALU.mult, ALU.add,
                accum_out=mc),
               reads=[src_res], writes=[mres, "dmp"], war=war0)
        else:
            mc, mres = m_cell, m_res
        op("dve", lambda e: e.scalar_tensor_tensor(
            dmp, src_ap, 1.0 / D, src_ap, ALU.mult, ALU.mult,
            accum_out=cell(3)),
           reads=[src_res], writes=[r("e2"), "dmp"],
           war=(war0 if m_cell is not None else ()))
        op("dve", lambda e, mc=mc: e.tensor_scalar(cell(4), mc, mc, -EPS,
                                                   ALU.mult, ALU.add),
           reads=[mres], writes=[r("v")])
        op("dve", lambda e: e.tensor_tensor(s_st[:, b + 4:b + 5], cell(3),
                                            cell(4), ALU.subtract),
           reads=[r("e2"), r("v")], writes=[r("v")])
        op("act", lambda e: e.activation(cell(5), cell(4), AF.Ln,
                                         bias=0.0, scale=1.0),
           reads=[r("v")], writes=[r("l")])
        op("act", lambda e: e.activation(cell(5), cell(5), AF.Exp,
                                         bias=0.0, scale=-0.5),
           reads=[r("l")], writes=[r("rs")])
        op("dve", lambda e, mc=mc: e.tensor_scalar(cell(6), mc, cell(5), -1.0,
                                                   ALU.mult, ALU.mult),
           reads=[mres, r("rs")], writes=[r("n")])
        op("act", lambda e: e.activation(dst_ap, src_ap, AF.Identity,
                                         bias=cell(6), scale=cell(5)),
           reads=[src_res, r("rs"), r("n")], writes=[dst_res], war=dst_war)

    def p1_L(j):
        par = j % 2
        xs = s_x[:, D * j:D * (j + 1)]
        xn = s_ln[:, 1024 * par:1024 * (par + 1)]
        emit_ln(xs, "x%d" % j, xn, "lnr%d" % par, pid=j % 4, eng="dve",
                m_out=(s_sx[:, j:j + 1], "mx%d" % j))

    def p1_T(j):
        par = j % 2
        xn = s_ln[:, 1024 * par:1024 * (par + 1)]
        pb = pO[:, 512 * par:512 * (par + 1)].bitcast(BF16)
        bank = ["ps%d" % (4 + par)]
        for c in range(ND):
            op("pe",
               lambda e, c=c, xn=xn, pb=pb: e.transpose(
                   pb[:, 128 * c:128 * (c + 1)],
                   xn[:, 128 * c:128 * (c + 1)], IDT),
               reads=["lnr%d" % par, "cst"],
               writes=(bank if c == 0 else []),
               bump=(bank if c > 0 else []))
        dst = s_f[:].rearrange("p (c n) -> p c n", c=ND)[:, :, 128 * j:128 * (j + 1)]
        src = pb[:, 0:1024].rearrange("p (c n) -> p c n", c=ND)
        op("act", lambda e, dst=dst, src=src: e.copy(dst, src),
           reads=bank, writes=["xnT%d" % j])


    def wslot(slot):
        if slot < 4:
            return s_wb[:, 1024 * slot:1024 * (slot + 1)]
        return s_wd[:, 1024 * (slot - 4):1024 * (slot - 3)]

    def w_chunk_dma(dram, c, slot, queue="qpool"):
        src = dram.rearrange("(kc p) n -> p kc n", p=128)[:, :, 128 * c:128 * (c + 1)]
        dstv = wslot(slot).rearrange("p (kc n) -> p kc n", n=128)
        op("pool", lambda e, src=src, dstv=dstv: e.dma_start(dstv, src),
           writes=["wb%d" % slot], queue=queue)

    def fm_matmuls(slot, wres_list, reads_fn, out_view_fn, psum_res_fn,
                   src_fn=None):
        if src_fn is None:
            src_fn = xfm
        for si in range(4):
            for kc in range(ND):
                first = kc == 0
                op("pe",
                   lambda e, kc=kc, si=si, slot=slot, first=first:
                   e.matmul(out_view_fn(si),
                            wslot(slot)[:, 128 * kc:128 * (kc + 1)],
                            src_fn(kc)[:, 512 * si:512 * (si + 1)],
                            start=first, stop=(kc == ND - 1)),
                   reads=wres_list + reads_fn(si),
                   writes=([psum_res_fn(si)] if first else []),
                   bump=([] if first else [psum_res_fn(si)]))

    norm_pend = []

    def norm_sq(which, c, si, src_ap, dres):
        par = q2par[0]
        q2par[0] ^= 1
        q2 = s_wd[:, 2048 + 512 * par:2048 + 512 * (par + 1)]
        op("pool", lambda e, src_ap=src_ap, q2=q2: e.tensor_mul(
            q2, src_ap, src_ap),
           reads=[dres], writes=["q2%d" % par])
        bcol = (0 if which == "q" else 32) + 4 * c + si
        norm_pend.append((par, bcol))

    def norm_flush(n=1):
        for _ in range(n):
            if not norm_pend:
                return
            par, bcol = norm_pend.pop(0)
            nrm = pS[64 * par:64 * par + 1, 512:1024]
            q2 = s_wd[:, 2048 + 512 * par:2048 + 512 * (par + 1)]
            op("pe", lambda e, q2=q2, nrm=nrm, par=par: e.matmul(
                nrm, ONES[:, 0:1], q2, start=True, stop=True,
                tile_position=(0, 64 * par), skip_group_check=True),
               reads=["q2%d" % par, "cst"], writes=["ps1"])
            op("dve", lambda e, bcol=bcol, nrm=nrm: e.reduce_max(
                s_Bw[0:1, bcol:bcol + 1], nrm, axis=AX.X),
               reads=["ps1"], writes=["Bw%d" % bcol])

    ubank = [0]
    q2par = [0]

    fifo = []

    def fifo_step(k=1):
        while fifo and k > 0:
            fifo.pop(0)()
            k -= 1

    def fifo_drain():
        while fifo:
            fifo.pop(0)()

    def push_qk_unit(which, c, si):
        bias_t, bias_res, dest_fn = (
            (s_bq, "bq", q_fm) if which == "q" else (s_bk, "bk", k_fm))
        slot = (3 * c + (0 if which == "q" else 1)) % 6
        out = pS[:, 0:512]

        def piece(kc0):
            if kc0 == 0 and len(norm_pend) >= 2:
                norm_flush(1)
            for kc in (kc0, kc0 + 1):
                first = kc == 0
                op("pe",
                   lambda e, kc=kc, si=si, slot=slot, first=first, out=out:
                   e.matmul(out, wslot(slot)[:, 128 * kc:128 * (kc + 1)],
                            xfm(kc)[:, 512 * si:512 * (si + 1)],
                            start=first, stop=(kc == ND - 1),
                            skip_group_check=True),
                   reads=["wb%d" % slot] + ["xnT%d" % (4 * si + k)
                                            for k in range(4)],
                   writes=(["ps0"] if first else []),
                   bump=([] if first else ["ps0"]))
            if kc0 == ND - 2:
                dres = "%s_%d_%d" % (which, c, si)
                dst = dest_fn(c)[:, 512 * si:512 * (si + 1)]
                if c == 0:
                    op("act",
                       lambda e, dst=dst, out=out, bias_t=bias_t, c=c:
                       e.activation(dst, out, AF.Identity,
                                    bias=bias_t[:, c:c + 1], scale=1.0),
                       reads=["ps0", bias_res], writes=[dres])
                else:
                    op("dve",
                       lambda e, dst=dst, out=out, bias_t=bias_t, c=c:
                       e.tensor_scalar(dst, out, bias_t[:, c:c + 1], None,
                                       ALU.add),
                       reads=["ps0", bias_res], writes=[dres])
                norm_sq(which, c, si, dst, dres)
        for kc0 in range(0, ND, 2):
            fifo.append(lambda kc0=kc0: piece(kc0))

    def emit_qk_unit(which, c, si):
        push_qk_unit(which, c, si)
        fifo_drain()

    def emit_B_calc(c):
        op("dve", lambda e, c=c: e.reduce_max(
            s_Bw[0:1, 64 + c:65 + c], s_Bw[0:1, 4 * c:4 * c + 4], axis=AX.X),
           reads=["Bw%d" % (4 * c + i) for i in range(4)], writes=["qm%d" % c])
        op("dve", lambda e, c=c: e.reduce_max(
            s_Bw[0:1, 72 + c:73 + c], s_Bw[0:1, 32 + 4 * c:36 + 4 * c],
            axis=AX.X),
           reads=["Bw%d" % (32 + 4 * c + i) for i in range(4)],
           writes=["km%d" % c])
        op("dve", lambda e, c=c: e.tensor_tensor(
            s_Bw[0:1, 80 + c:81 + c], s_Bw[0:1, 64 + c:65 + c],
            s_Bw[0:1, 72 + c:73 + c], ALU.mult),
           reads=["qm%d" % c, "km%d" % c], writes=["Bqk%d" % c])
        op("act", lambda e, c=c: e.activation(
            s_Bw[0:1, 88 + c:89 + c], s_Bw[0:1, 80 + c:81 + c], AF.Ln,
            bias=0.0, scale=1.0),
           reads=["Bqk%d" % c], writes=["Bln%d" % c])
        op("act", lambda e, c=c: e.activation(
            s_Bw[0:1, 96 + c:97 + c], s_Bw[0:1, 88 + c:89 + c], AF.Exp,
            bias=0.0, scale=0.5),
           reads=["Bln%d" % c], writes=["Bex%d" % c])
        nbrow = s_wd[0:1, 3072 + c:3073 + c]
        op("dve", lambda e, c=c, nbrow=nbrow: e.tensor_scalar(
            nbrow, s_Bw[0:1, 96 + c:97 + c], -0.125, None, ALU.mult),
           reads=["Bex%d" % c], writes=["nbrow%d" % c])

    def emit_B_nbp(c):
        nbrow = s_wd[0:1, 3072 + c:3073 + c]
        nbp = pS[:, 0:1]
        i1 = op("pe", lambda e, nbrow=nbrow, nbp=nbp: e.matmul(
            nbp[0:64, :], ONES[0:1, 0:64], nbrow, start=True, stop=True,
            skip_group_check=True),
            reads=["nbrow%d" % c, "cst"], writes=["ps0"])
        op("pe", lambda e, nbrow=nbrow, nbp=nbp: e.matmul(
            nbp[64:128, :], ONES[0:1, 0:64], nbrow, start=True, stop=True,
            tile_position=(0, 64), skip_group_check=True),
           reads=["nbrow%d" % c, "cst"], deps=[i1], bump=["ps0"])
        op("dve", lambda e, c=c, nbp=nbp: e.tensor_copy(
            s_nB[:, c:c + 1], nbp),
           reads=["ps0"], writes=["nB%d" % c])

    def emit_v_round(c, si):
        slot = (3 * c + 2) % 6
        bank = ubank[0]
        ubank[0] ^= 1
        bres = "ps%d" % bank
        out = pS[:, 512 * bank:512 * (bank + 1)]
        for kc in range(ND):
            first = kc == 0
            op("pe",
               lambda e, kc=kc, si=si, slot=slot, first=first, out=out:
               e.matmul(out, wslot(slot)[:, 128 * kc:128 * (kc + 1)],
                        xfm(kc)[:, 512 * si:512 * (si + 1)],
                        start=first, stop=(kc == ND - 1)),
               reads=["wb%d" % slot] + ["xnT%d" % (4 * si + k) for k in range(4)],
               writes=([bres] if first else []),
               bump=([] if first else [bres]))
        vstage = s_at[:, 2048 * (c % 2):2048 * (c % 2) + 2048]
        atres = "at%d" % (c % 2)
        if c == 0:
            op("act",
               lambda e, si=si, c=c, vstage=vstage, out=out: e.activation(
                   vstage[:, 512 * si:512 * (si + 1)], out, AF.Identity,
                   bias=s_bv[:, c:c + 1], scale=1.0),
               reads=[bres, "bv"],
               writes=([atres] if si == 0 else []),
               bump=([] if si == 0 else [atres]))
        else:
            op("dve",
               lambda e, si=si, c=c, vstage=vstage, out=out: e.tensor_scalar(
                   vstage[:, 512 * si:512 * (si + 1)], out,
                   s_bv[:, c:c + 1], None, ALU.add),
               reads=[bres, "bv"],
               writes=([atres] if si == 0 else []),
               bump=([] if si == 0 else [atres]))
        pb = out.bitcast(BF16)
        for jj in range(4):
            j = 4 * si + jj
            op("pe",
               lambda e, j=j, jj=jj, pb=pb, vstage=vstage: e.transpose(
                   pb[:, 128 * jj:128 * (jj + 1)],
                   vstage[:, 128 * j:128 * (j + 1)], IDT),
               reads=[atres, "cst"],
               writes=([bres] if jj == 0 else []),
               bump=([] if jj == 0 else [bres]))
        for jj in range(4):
            j = 4 * si + jj
            for hh in range(2):
                h = 2 * c + hh
                dst = s_b[:, V0 + 1040 * j + 65 * h:
                          V0 + 1040 * j + 65 * h + 64]
                op("dve",
                   lambda e, dst=dst, hh=hh, jj=jj, pb=pb: e.tensor_copy(
                       dst, pb[:, 128 * jj + 64 * hh:128 * jj + 64 * hh + 64]),
                   reads=[bres, "vones"],
                   writes=(["vx_%d_%d" % (j, c)] if hh == 0 else []),
                   bump=([] if hh == 0 else ["vx_%d_%d" % (j, c)]))

    op("pool", lambda e: e.memset(s_b[:, V0:V0 + 1040 * NJ + 128], 1.0),
       writes=["vones"])

    def emit_norm_half(h, hf, part):
        c = h // 2
        lo = 1024 * hf
        bks = ["ps%d" % (4 + 2 * hf), "ps%d" % (5 + 2 * hf)]
        o_res = "o_h%d_%d" % (h, hf)
        osc = s_ln[0:64, lo:lo + 1024]
        rr = s_dmp[64:65, lo:lo + 1024]
        if part == 0:
            op("dve", lambda e, osc=osc, lo=lo: e.tensor_copy(
                osc, pO[0:64, lo:lo + 1024]),
               reads=bks, writes=["osc%d" % hf],
               war=(["lnr0", "lnr1"] if h == 0 else ()))
            op("act", lambda e, lo=lo: e.activation(
                pO[64:65, lo:lo + 1024], pO[64:65, lo:lo + 1024], AF.Ln,
                bias=0.0, scale=1.0),
               reads=[bks[0]], writes=bks)
            op("act", lambda e, rr=rr, lo=lo: e.activation(
                rr, pO[64:65, lo:lo + 1024], AF.Exp, bias=0.0, scale=-1.0),
               reads=bks, writes=["rrq%d" % hf], bump=["dmp"],
               war=(["dmp"] if h == 0 else ()))
        else:
            i1 = op("pe", lambda e, rr=rr, lo=lo: e.matmul(
                pO[0:64, lo:lo + 512], s_cst[64:65, 384:448], rr[:, 0:512],
                start=True, stop=True, skip_group_check=True),
                reads=["rrq%d" % hf, "osc%d" % hf], writes=[bks[0]])
            op("pe", lambda e, rr=rr, lo=lo: e.matmul(
                pO[0:64, lo + 512:lo + 1024], s_cst[64:65, 384:448],
                rr[:, 512:1024],
                start=True, stop=True, skip_group_check=True),
                reads=["rrq%d" % hf, "osc%d" % hf], writes=[bks[1]])
            qres2 = ["q_%d_%d" % (c, 2 * hf), "q_%d_%d" % (c, 2 * hf + 1)]
            if h % 2 == 0:
                dst = ofm(c)[0:64, lo:lo + 1024]
                op("dve", lambda e, dst=dst, osc=osc, lo=lo: e.tensor_tensor(
                    dst, osc, pO[0:64, lo:lo + 1024], ALU.mult),
                   reads=["osc%d" % hf] + bks, writes=[o_res],
                   war=qres2, bump=qres2)
            else:
                onorm = s_dmp[0:64, lo:lo + 1024]
                op("dve", lambda e, onorm=onorm, osc=osc, lo=lo:
                   e.tensor_tensor(onorm, osc, pO[0:64, lo:lo + 1024],
                                   ALU.mult),
                   reads=["osc%d" % hf] + bks, writes=["onq%d" % hf],
                   bump=["dmp"],
                   war=(["dmp"] if h == 1 else ()))
                op("sp", lambda e, c=c, onorm=onorm, lo=lo: e.dma_start(
                    ofm(c)[64:128, lo:lo + 1024], onorm),
                   reads=["onq%d" % hf], writes=[o_res], queue="qsp",
                   war=qres2)

    kpadv = [s_y2[:, 0:1024].bitcast(BF16), s_y2[:, 1024:2048].bitcast(BF16)]
    op("dve", lambda e: e.memset(kpadv[0][64:128, :], 0.0), writes=["kpz0"])
    op("dve", lambda e: e.memset(kpadv[1][0:64, :], 0.0), writes=["kpz1"])
    pend_av = []
    pend_n1 = [None]
    sctr = [1]

    def emit_av_chunk(ph, pk, q0, n):
        c = ph // 2
        par = pk % 2
        atres = "at%d" % par
        first = pk == 0
        lastk = min((q0 + n - 1) // 128, NJ - 1)
        banks = psb(pO, q0, q0 + n)
        op("pe",
           lambda e, j=pk, h=ph, q0=q0, n=n, par=par, first=first,
           stopf=(pk == lastk):
           e.matmul(pO[:, q0:q0 + n],
                    s_b[:, V0 + 1040 * j + 65 * h:
                        V0 + 1040 * j + 65 * h + 128],
                    s_at[:, 2048 * par + (q0 - 128 * j):
                         2048 * par + (q0 - 128 * j) + n],
                    start=first, stop=stopf, skip_group_check=True),
           reads=[atres, "vx_%d_%d" % (pk, c), "vones"],
           writes=(banks if first else []),
           bump=([] if first else banks))

    def emit_pend_batch(pv):
        ph, pk, pch = pv
        for (q0, n) in pch:
            emit_av_chunk(ph, pk, q0, n)

        def post():
            if pk == NJ - 1:
                emit_norm_half(ph, 1, 0)
                pend_n1[0] = ph
            elif pk == 7:
                emit_norm_half(ph, 0, 0)
                emit_norm_half(ph, 0, 1)
        return post

    def emit_attn_step(h, ktc):
        c = h // 2
        hp = h % 2
        if ktc == 8 and 1 <= h < NH - 1:
            emit_kpad_copy(h + 1)
        post = None
        if len(pend_av) == 2:
            if pend_av[0][1] == 0 and pend_n1[0] is not None:
                emit_norm_half(pend_n1[0], 1, 1)
                pend_n1[0] = None
            post = emit_pend_batch(pend_av.pop(0))
        defer_n1 = pend_n1[0] is not None
        qres = ["q_%d_%d" % (c, si) for si in range(4)]
        par = ktc % 2
        atres = "at%d" % par
        chunks = []
        q0 = 128 * ktc
        while q0 < S:
            n = min(512 - (q0 % 512), S - q0)
            chunks.append((q0, n))
            q0 += n
        first_exp = True
        gi = 0
        while gi < len(chunks):
            lo = 0 if not fifo else 1
            s = sctr[0]
            if s < lo:
                s = lo
            g = 2 if (gi + 1 < len(chunks)
                      and chunks[gi + 1][1] == chunks[gi][1]
                      and s != 3) else 1
            sctr[0] = lo + (s + g - lo) % (4 - lo)
            for u in range(g):
                q0, n = chunks[gi + u]
                sl = s + u
                diag = gi + u == 0
                op("pe",
                   lambda e, sl=sl, q0=q0, n=n, ktc=ktc, hp=hp, c=c,
                   diag=diag:
                   e.matmul(pS[:, 512 * sl:512 * sl + n],
                            kpadv[hp][:, 128 * ktc:128 * (ktc + 1)],
                            q_fm(c)[:, q0:q0 + n],
                            start=True, stop=(not diag),
                            skip_group_check=True),
                   reads=["kp%d" % hp] + qres, writes=["ps%d" % sl])
                if diag:
                    op("pe",
                       lambda e, sl=sl: e.matmul(
                           pS[:, 512 * sl:512 * sl + 128], NEGI, LMSK,
                           start=False, stop=True, skip_group_check=True),
                       reads=["cst"], bump=["ps%d" % sl])
            q0, n = chunks[gi]
            dof = 2048 * par + (q0 - 128 * ktc)
            if g == 2:
                src = pS[:, 512 * s:512 * (s + 2)].rearrange(
                    "p (g n) -> p g n", g=2)[:, :, 0:n]
                dst = s_at[:, dof:dof + 2 * n].rearrange(
                    "p (g n) -> p g n", g=2)
            else:
                src = pS[:, 512 * s:512 * s + n]
                dst = s_at[:, dof:dof + n]
            op("act",
               lambda e, src=src, dst=dst, c=c: e.activation(
                   dst, src, AF.Exp, bias=s_nB[:, c:c + 1], scale=QSCALE),
               reads=["nB%d" % c] + ["ps%d" % (s + k) for k in range(g)],
               writes=([atres] if first_exp else []),
               bump=([] if first_exp else [atres]))
            first_exp = False
            fifo_step(2 if defer_n1 else 1)
            gi += g
        if defer_n1:
            emit_norm_half(pend_n1[0], 1, 1)
            pend_n1[0] = None
        if post is not None:
            post()
        pend_av.append((h, ktc, chunks))

    def emit_kpad_copy(h):
        c = h // 2
        rbase = 64 * (h % 2)
        hp = h % 2
        kres = ["k_%d_%d" % (c, si) for si in range(4)]
        op("dve",
           lambda e, hp=hp, c=c, rbase=rbase: e.tensor_copy(
               kpadv[hp][rbase:rbase + 64, :], k_fm(c)[rbase:rbase + 64, :]),
           reads=kres + ["kpz%d" % hp], writes=["kp%d" % hp])

    def attention_drain():
        while pend_av:
            if pend_n1[0] is not None:
                emit_norm_half(pend_n1[0], 1, 1)
                pend_n1[0] = None
            emit_pend_batch(pend_av.pop(0))()

    for t in range(NJ + 1):
        if t < NJ:
            if t % 2 == 0 and t // 2 + 2 < NJ // 2:
                x_dma(t // 2 + 2)
            p1_L(t)
            if t == 0:
                w_chunk_dma(dWk, 0, 1)
                w_chunk_dma(dWv, 0, 2)
            elif t == 2:
                w_chunk_dma(dWq, 1, 3)
            elif t == 4:
                w_chunk_dma(dWk, 1, 4)
            elif t == 6:
                w_chunk_dma(dWv, 1, 5)
        if t >= 1:
            j = t - 1
            p1_T(j)
            if j % 4 == 3:
                si = j // 4
                emit_qk_unit("q", 0, si)
                emit_qk_unit("k", 0, si)
                emit_v_round(0, si)
    norm_flush(99)
    emit_B_calc(0)
    emit_B_nbp(0)
    emit_kpad_copy(0)
    emit_kpad_copy(1)
    w_chunk_dma(dWq, 2, 0)
    w_chunk_dma(dWk, 2, 1)
    w_chunk_dma(dWv, 2, 2)
    if trunc_step is not None and trunc_step <= 0:
        P.emit_all(final_deps=[])
        return nc

    for c in range(1, ND):
        steps = [(h, ktc) for h in (2 * c - 2, 2 * c - 1) for ktc in range(NJ)]
        for w, si in [(w, si) for si in range(4) for w in ("q", "k")]:
            push_qk_unit(w, c, si)
        fifo.append(lambda c=c: (norm_flush(99), emit_B_calc(c)))
        fifo.append(lambda c=c: emit_B_nbp(c))
        for h, ktc in steps:
            emit_attn_step(h, ktc)
        fifo_drain()
        attention_drain()
        for si in range(4):
            emit_v_round(c, si)
        if c + 2 < ND:
            w_chunk_dma(dWq, c + 2, (3 * (c + 2)) % 6)
            w_chunk_dma(dWk, c + 2, (3 * (c + 2) + 1) % 6)
            w_chunk_dma(dWv, c + 2, (3 * (c + 2) + 2) % 6)
        if trunc_step is not None and c >= trunc_step:
            P.emit_all(final_deps=[])
            return nc
    for h in (NH - 2, NH - 1):
        for ktc in range(NJ):
            emit_attn_step(h, ktc)
    attention_drain()
    emit_norm_half(NH - 1, 1, 1)
    pend_n1[0] = None
    if trunc:
        P.emit_all(final_deps=[])
        return nc

    for c in range(ND):
        slot = c % 2
        w_chunk_dma(dWo, c, slot)
        fm_matmuls(slot, ["wb%d" % slot],
                   lambda si: ["o_h%d_%d" % (h2, si // 2) for h2 in range(NH)],
                   acc_view, lambda si: "ps%d" % si, src_fn=ofm)
        for si in range(4):
            op("act",
               lambda e, c=c, si=si: e.activation(
                   s_f[:, S * c + 512 * si: S * c + 512 * (si + 1)],
                   acc_view(si), AF.Identity, bias=s_bo[:, c:c + 1], scale=1.0),
               reads=["ps%d" % si, "bo"], writes=["zf_%d_%d" % (c, si)],
               war=["xnT%d" % (4 * si + k) for k in range(4)])

    def p3_A(j):
        par = j % 2
        bank1 = "ps%d" % (4 + par)
        pb = pO[:, 512 * par:512 * (par + 1)].bitcast(BF16)
        for c in range(ND):
            op("pe",
               lambda e, c=c, j=j, pb=pb: e.transpose(
                   pb[:, 128 * c:128 * (c + 1)],
                   s_f[:, S * c + 128 * j: S * c + 128 * (j + 1)], IDT),
               reads=["zf_%d_%d" % (c, j // 4), "cst"],
               writes=([bank1] if c == 0 else []),
               bump=([] if c == 0 else [bank1]))
        ztm = s_ln[:, 1024 * par:1024 * (par + 1)]
        op("dve", lambda e, pb=pb, ztm=ztm: e.tensor_copy(ztm, pb[:, 0:1024]),
           reads=[bank1], writes=["lnr%d" % par],
           war=["osc0", "osc1"])
        yn = s_at[:, 1024 * par:1024 * (par + 1)]
        emit_ln(ztm, "lnr%d" % par, yn, "yn%d" % par, pid=j % 4, eng="dve",
                dst_war=("at0",) if j < 2 else (),
                war0=(["rrq0", "rrq1", "onq0", "onq1"] if j == 0 else ()))

    def p3_B(j):
        par = j % 2
        yn = s_at[:, 1024 * par:1024 * (par + 1)]
        op("pool",
           lambda e, j=j, yn=yn: e.tensor_tensor(
               s_x[:, D * j:D * (j + 1)], s_x[:, D * j:D * (j + 1)], yn,
               ALU.add),
           reads=["yn%d" % par, "x%d" % j], writes=["x%d" % j])

    def p3_C(j):
        par = j % 2
        yn2 = s_at[:, 2048 + 1024 * par:2048 + 1024 * (par + 1)]
        emit_ln(s_x[:, D * j:D * (j + 1)], "x%d" % j, yn2, "yn2%d" % par,
                pid=4 + j % 4, eng="dve",
                m_cell=s_sx[:, j:j + 1], m_res="mx%d" % j,
                dst_war=("at1",) if j < 2 else ())

    def p3_D(j):
        par = j % 2
        yn2 = s_at[:, 2048 + 1024 * par:2048 + 1024 * (par + 1)]
        bank2 = "ps%d" % (6 + par)
        pb2 = pO[:, 1024 + 512 * par:1536 + 512 * par].bitcast(BF16)
        for c in range(ND):
            op("pe",
               lambda e, c=c, pb2=pb2, yn2=yn2: e.transpose(
                   pb2[:, 128 * c:128 * (c + 1)],
                   yn2[:, 128 * c:128 * (c + 1)], IDT),
               reads=["yn2%d" % par, "cst"],
               writes=([bank2] if c == 0 else []),
               bump=([] if c == 0 else [bank2]))
        dst = s_f[:].rearrange("p (c n) -> p c n", c=ND)[:, :, 128 * j:128 * (j + 1)]
        src = pb2[:, 0:1024].rearrange("p (c n) -> p c n", c=ND)
        op("act", lambda e, dst=dst, src=src: e.copy(dst, src),
           reads=[bank2], writes=["ynT%d" % j],
           war=["zf_%d_%d" % (c2, j // 4) for c2 in range(ND)])

    for t in range(NJ + 3):
        if t < NJ:
            p3_A(t)
        if 0 <= t - 1 < NJ:
            p3_B(t - 1)
        if 0 <= t - 2 < NJ:
            p3_C(t - 2)
        if 0 <= t - 3 < NJ:
            p3_D(t - 3)

    for hc in range(NHC):
        gslot = hc % 2
        uslot = 2 + hc % 2
        w_chunk_dma(dWg, hc, gslot)
        w_chunk_dma(dWu, hc, uslot)
        hwar = []
        if hc < ND:
            hwar = ["zf_%d_%d" % (hc, s2) for s2 in range(4)]
        elif hc < 2 * ND:
            hwar = ["k_%d_%d" % (hc - ND, s2) for s2 in range(4)]
        elif hc == 2 * ND:
            hwar = (["vx_%d_%d" % (j2, c2) for j2 in range(NJ)
                     for c2 in range(ND)] + ["vones"])
        for si in range(4):
            for kc in range(ND):
                first = kc == 0
                op("pe",
                   lambda e, kc=kc, si=si, gslot=gslot, first=first:
                   e.matmul(acc_view(si),
                            wslot(gslot)[:, 128 * kc:128 * (kc + 1)],
                            xfm(kc)[:, 512 * si:512 * (si + 1)],
                            start=first, stop=(kc == ND - 1)),
                   reads=["wb%d" % gslot] + ["ynT%d" % (4 * si + k) for k in range(4)],
                   writes=(["ps%d" % si] if first else []),
                   bump=([] if first else ["ps%d" % si]))
            for kc in range(ND):
                first = kc == 0
                op("pe",
                   lambda e, kc=kc, si=si, uslot=uslot, first=first:
                   e.matmul(pO[:, 512 * si:512 * (si + 1)],
                            wslot(uslot)[:, 128 * kc:128 * (kc + 1)],
                            xfm(kc)[:, 512 * si:512 * (si + 1)],
                            start=first, stop=(kc == ND - 1)),
                   reads=["wb%d" % uslot] + ["ynT%d" % (4 * si + k) for k in range(4)],
                   writes=(["ps%d" % (4 + si)] if first else []),
                   bump=([] if first else ["ps%d" % (4 + si)]))
            stg = s_dmp[:, 1024 + 512 * (si % 2): 1536 + 512 * (si % 2)]
            op("act",
               lambda e, si=si, stg=stg: e.activation(
                   stg, acc_view(si), AF.Silu, bias=0.0, scale=1.0 / 127.0),
               reads=["ps%d" % si], writes=["silu%d" % (si % 2)],
               war=(("rrq1", "onq1") if (hc == 0 and si < 2) else ()))
            op("dve",
               lambda e, hc=hc, si=si, stg=stg: e.tensor_tensor(
                   s_b[:, S * hc + 512 * si: S * hc + 512 * (si + 1)],
                   stg, pO[:, 512 * si:512 * (si + 1)], ALU.mult),
               reads=["silu%d" % (si % 2), "ps%d" % (4 + si)],
               writes=["h_%d_%d" % (hc, si)],
               war=hwar if si == 0 else ())

    def wd_dma(c):
        slot = c % 2
        src = dWd.rearrange("(hc p) n -> p hc n", p=128)[:, :, 128 * c:128 * (c + 1)]
        dstv = s_wd[:, 3072 * slot:3072 * (slot + 1)].rearrange(
            "p (hc n) -> p hc n", n=128)
        op("pool", lambda e, src=src, dstv=dstv: e.dma_start(dstv, src),
           writes=["wd%d_0" % slot, "wd%d_1" % slot], queue="qx",
           war=((["wb4", "wb5", "q20", "q21"]
                 + ["nbrow%d" % c2 for c2 in range(ND)]) if c < 2 else ()))

    wd_dma(0)
    wd_dma(1)
    for c in range(ND):
        slot = c % 2
        for si in range(4):
            for hc in range(NHC):
                first = hc == 0
                op("pe",
                   lambda e, hc=hc, si=si, slot=slot, first=first:
                   e.matmul(acc_view(si),
                            s_wd[:, 3072 * slot + 128 * hc:
                                 3072 * slot + 128 * (hc + 1)],
                            s_b[:, S * hc + 512 * si: S * hc + 512 * (si + 1)],
                            start=first, stop=(hc == NHC - 1)),
                   reads=["wd%d_%d" % (slot, hc // 12), "h_%d_%d" % (hc, si)],
                   writes=(["ps%d" % si] if first else []),
                   bump=([] if first else ["ps%d" % si]))
            op("act",
               lambda e, c=c, si=si: e.activation(
                   s_f[:, S * c + 512 * si: S * c + 512 * (si + 1)],
                   acc_view(si), AF.Identity, bias=0.0, scale=FFS),
               reads=["ps%d" % si], writes=["df_%d_%d" % (c, si)],
               war=(ALLYNT if (c == 0 and si == 0) else ()))
        if c + 2 < ND:
            wd_dma(c + 2)

    oY = []

    def p6_T(j):
        par = j % 2
        bank = "ps%d" % (4 + par)
        pb = pO[:, 512 * par:512 * (par + 1)].bitcast(BF16)
        for c in range(ND):
            op("pe",
               lambda e, c=c, j=j, pb=pb: e.transpose(
                   pb[:, 128 * c:128 * (c + 1)],
                   s_f[:, S * c + 128 * j: S * c + 128 * (j + 1)], IDT),
               reads=["df_%d_%d" % (c, j // 4), "cst"],
               writes=([bank] if c == 0 else []),
               bump=([] if c == 0 else [bank]))
        dtm = s_ln[:, 1024 * par:1024 * (par + 1)]
        op("act", lambda e, pb=pb, dtm=dtm: e.copy(dtm, pb[:, 0:1024]),
           reads=[bank], writes=["lnr%d" % par])

    def p6_L(j):
        par = j % 2
        dtm = s_ln[:, 1024 * par:1024 * (par + 1)]
        y2 = s_y2[:, 1024 * par:1024 * (par + 1)]
        emit_ln(dtm, "lnr%d" % par, y2, "y2_%d" % par, pid=j % 4, eng="dve",
                dst_war=("kp%d" % par, "kpz%d" % par) if j < 2 else ())

    def p6_AD(j):
        par = j % 2
        y2 = s_y2[:, 1024 * par:1024 * (par + 1)]
        y2r = "y2_%d" % par
        if j % 2:
            op("pool",
               lambda e, j=j, y2=y2: e.tensor_tensor(
                   y2, y2, s_x[:, D * j:D * (j + 1)], ALU.add),
               reads=[y2r, "x%d" % j], writes=[y2r])
        else:
            op("dve",
               lambda e, j=j, y2=y2: e.tensor_tensor(
                   y2, y2, s_x[:, D * j:D * (j + 1)], ALU.add),
               reads=[y2r, "x%d" % j], writes=[y2r])
        yq = ("qsp", "qsp2", "qsp3")[j % 3]
        oY.append(op("sp",
                     lambda e, j=j, y2=y2: e.dma_start(
                         dY[128 * j:128 * (j + 1), :], y2),
                     reads=[y2r], writes=["yout%d" % j], queue=yq))

    for t in range(NJ + 2):
        if t < NJ:
            p6_T(t)
        if 0 <= t - 1 < NJ:
            p6_L(t - 1)
        if 0 <= t - 2 < NJ:
            p6_AD(t - 2)

    P.emit_all(final_deps=oY)
    return nc


def kernel(L0_attn_X, L0_attn_Wq, L0_attn_bq, L0_attn_Wk, L0_attn_bk,
           L0_attn_Wv, L0_attn_bv, L0_attn_Wo, L0_attn_bo,
           L0_ffn_w_gate, L0_ffn_w_up, L0_ffn_w_down):
    X = np.ascontiguousarray(np.asarray(L0_attn_X, dtype=np.float32))
    wq = np.ascontiguousarray(np.asarray(L0_attn_Wq, dtype=np.float32))
    wk = np.ascontiguousarray(np.asarray(L0_attn_Wk, dtype=np.float32))
    wv = np.ascontiguousarray(np.asarray(L0_attn_Wv, dtype=np.float32))
    wo = np.ascontiguousarray(np.asarray(L0_attn_Wo, dtype=np.float32))
    bqr = np.ascontiguousarray(np.asarray(L0_attn_bq, np.float32).reshape(ND, 128).T)
    bkr = np.ascontiguousarray(np.asarray(L0_attn_bk, np.float32).reshape(ND, 128).T)
    bvr = np.ascontiguousarray(np.asarray(L0_attn_bv, np.float32).reshape(ND, 128).T)
    bor = np.ascontiguousarray(np.asarray(L0_attn_bo, np.float32).reshape(ND, 128).T)
    wg = np.ascontiguousarray(np.asarray(L0_ffn_w_gate, np.int32).T)
    wu = np.ascontiguousarray(np.asarray(L0_ffn_w_up, np.int32).T)
    wd = np.ascontiguousarray(np.asarray(L0_ffn_w_down, np.int32).T)

    consts = np.zeros((128, 448), dtype=ml_dtypes.bfloat16)
    consts[:, 0:128] = np.eye(128, dtype=np.float32).astype(ml_dtypes.bfloat16)
    consts[:, 128:256] = np.tril(np.ones((128, 128), np.float32), -1).astype(
        ml_dtypes.bfloat16)
    consts[:, 256:384] = (LNEG * np.eye(128, dtype=np.float32)).astype(
        ml_dtypes.bfloat16)
    consts[:, 384:448] = np.ones((128, 64), np.float32).astype(ml_dtypes.bfloat16)

    nc = build_program()
    shared = {"wq": wq, "wk": wk, "wv": wv, "wo": wo,
              "bqr": bqr, "bkr": bkr, "bvr": bvr, "bor": bor,
              "wg": wg, "wu": wu, "wd": wd, "consts": consts}
    in_maps = [dict(shared, x=np.ascontiguousarray(X[i])) for i in range(8)]
    res = run_bass_kernel_spmd(nc, in_maps, core_ids=list(range(8)))
    return np.stack([np.asarray(res.results[i]["y"], dtype=np.float32)
                     for i in range(8)], axis=0)


# version 39 (final)
# speedup vs baseline: 1.0052x; 1.0052x over previous
import numpy as np
import ml_dtypes
import concourse.bass as bass
import concourse.mybir as mybir
from concourse.bass_utils import run_bass_kernel_spmd

F32 = mybir.dt.float32
BF16 = mybir.dt.bfloat16
I32 = mybir.dt.int32
AF = mybir.ActivationFunctionType
ALU = mybir.AluOpType
AX = mybir.AxisListType

S = 2048
D = 1024
NH = 16
FF = 3072
NJ = S // 128
ND = D // 128
NHC = FF // 128
EPS = 1e-5
QSCALE = 0.125
FFS = 1.0 / (127.0 * 127.0)
LNEG = -1.0e9


class Prog:

    ENGINES = ("pe", "act", "dve", "pool", "sp")

    def __init__(self, nc):
        self.nc = nc
        self.ops = []
        self.res = {}
        self.sems = {name: nc.alloc_semaphore("s_" + name)
                     for name in ("pe", "act", "dve", "pool", "qsp", "qact",
                                  "qpool", "qx", "qx2", "qx3", "qx4",
                                  "qsp2", "qsp3")}

    def op(self, engine, emit, reads=(), writes=(), queue=None, war=(),
           deps=(), bump=()):
        idx = len(self.ops)
        dep = set(deps)
        for r in reads:
            e = self.res.get(r)
            if e is not None and e[0] is not None:
                dep.add(e[0])
        for w in tuple(writes) + tuple(war):
            e = self.res.get(w)
            if e is not None:
                if e[0] is not None:
                    dep.add(e[0])
                dep |= e[1]
        dep.discard(idx)
        self.ops.append({"engine": engine, "emit": emit, "deps": dep,
                         "queue": queue, "token": None})
        for r in reads:
            self.res.setdefault(r, [None, set()])[1].add(idx)
        for w in writes:
            self.res[w] = [idx, set()]
        for w in bump:
            self.res.setdefault(w, [None, set()])[0] = idx
        return idx

    def emit_all(self, final_deps):
        ops = self.ops
        needed = set(final_deps)
        for o in ops:
            needed |= o["deps"]
        cnt = {k: 0 for k in self.sems}
        for i, o in enumerate(ops):
            ch = o["queue"] if o["queue"] else o["engine"]
            if o["queue"] is not None or i in needed:
                inc = 16 if o["queue"] else 1
                cnt[ch] += inc
                o["token"] = (ch, cnt[ch], inc)
        per_eng = {e: [] for e in self.ENGINES}
        for i, o in enumerate(ops):
            per_eng[o["engine"]].append(i)
        sems = self.sems
        burst_end_val = {}
        burst_settle = {}
        for i, o in enumerate(ops):
            if o["queue"] is not None:
                _, val, _ = o["token"]
                burst_end_val[i] = val
                burst_settle[i] = val - 16

        def body(ename, eng):
            wm = {}

            def wait(ch, val):
                if wm.get(ch, 0) < val:
                    eng.wait_ge(sems[ch], val)
                    wm[ch] = val

            for i in per_eng[ename]:
                o = ops[i]
                if o["queue"] is not None and i in burst_settle:
                    wait(o["queue"], burst_settle[i])
                for d in sorted(o["deps"]):
                    ch, val, _ = ops[d]["token"]
                    if ops[d]["queue"] is not None:
                        val = burst_end_val[d]
                    wait(ch, val)
                inst = o["emit"](eng)
                if o["token"] is not None:
                    ch, val, inc = o["token"]
                    inst.then_inc(sems[ch], inc)
            if ename == "sp":
                for d in sorted(final_deps):
                    ch, val, _ = ops[d]["token"]
                    wait(ch, val)

        with self.nc.Block() as blk:
            blk.tensor(lambda e: body("pe", e))
            blk.scalar(lambda e: body("act", e))
            blk.vector(lambda e: body("dve", e))
            blk.gpsimd(lambda e: body("pool", e))
            blk.sync(lambda e: body("sp", e))


def build_program(trunc=0, trunc_step=None):
    nc = bass.Bass("TRN2", target_bir_lowering=False)

    dX = nc.dram_tensor("x", [S, D], F32, kind="ExternalInput").ap()
    dWq = nc.dram_tensor("wq", [D, D], F32, kind="ExternalInput").ap()
    dWk = nc.dram_tensor("wk", [D, D], F32, kind="ExternalInput").ap()
    dWv = nc.dram_tensor("wv", [D, D], F32, kind="ExternalInput").ap()
    dWo = nc.dram_tensor("wo", [D, D], F32, kind="ExternalInput").ap()
    dBq = nc.dram_tensor("bqr", [128, ND], F32, kind="ExternalInput").ap()
    dBk = nc.dram_tensor("bkr", [128, ND], F32, kind="ExternalInput").ap()
    dBv = nc.dram_tensor("bvr", [128, ND], F32, kind="ExternalInput").ap()
    dBo = nc.dram_tensor("bor", [128, ND], F32, kind="ExternalInput").ap()
    dWg = nc.dram_tensor("wg", [D, FF], I32, kind="ExternalInput").ap()
    dWu = nc.dram_tensor("wu", [D, FF], I32, kind="ExternalInput").ap()
    dWd = nc.dram_tensor("wd", [FF, D], I32, kind="ExternalInput").ap()
    dCst = nc.dram_tensor("consts", [128, 448], BF16, kind="ExternalInput").ap()
    dY = nc.dram_tensor("y", [S, D], F32, kind="ExternalOutput").ap()

    s_x = nc.alloc_sbuf_tensor("s_x", [128, NJ * D], BF16)
    s_f = nc.alloc_sbuf_tensor("s_f", [128, ND * S], BF16)
    s_b = nc.alloc_sbuf_tensor("s_b", [128, 49536], BF16)
    s_at = nc.alloc_sbuf_tensor("s_at", [128, 4096], BF16)
    s_ln = nc.alloc_sbuf_tensor("s_ln", [128, 2048], BF16)
    s_dmp = nc.alloc_sbuf_tensor("s_dmp", [128, 2048], BF16)
    s_y2 = nc.alloc_sbuf_tensor("s_y2", [128, 2048], F32)
    s_wb = nc.alloc_sbuf_tensor("s_wb", [128, 4096], BF16)
    s_wd = nc.alloc_sbuf_tensor("s_wd", [128, 6144], BF16)
    s_cst = nc.alloc_sbuf_tensor("s_cst", [128, 448], BF16)
    s_bq = nc.alloc_sbuf_tensor("s_bq", [128, ND], F32)
    s_bk = nc.alloc_sbuf_tensor("s_bk", [128, ND], F32)
    s_bv = nc.alloc_sbuf_tensor("s_bv", [128, ND], F32)
    s_bo = nc.alloc_sbuf_tensor("s_bo", [128, ND], F32)
    s_Bw = nc.alloc_sbuf_tensor("s_Bw", [1, 128], F32)
    s_nB = nc.alloc_sbuf_tensor("s_nB", [128, ND], F32)
    s_st = nc.alloc_sbuf_tensor("s_st", [128, 128], F32)
    s_sx = nc.alloc_sbuf_tensor("s_sx", [128, NJ], F32)

    pS = nc.alloc_psum_tensor("pS", [128, 2048], F32)
    pO = nc.alloc_psum_tensor("pO", [128, 2048], F32)

    P = Prog(nc)
    op = P.op

    IDT = s_cst[:, 0:128]
    LMSK = s_cst[:, 128:256]
    NEGI = s_cst[:, 256:384]
    ONES = s_cst[:, 384:448]

    Q0, K0, V0 = 0, ND * S, 2 * ND * S

    def q_fm(c):
        return s_b[:, Q0 + S * c: Q0 + S * (c + 1)]

    def k_fm(c):
        return s_b[:, K0 + S * c: K0 + S * (c + 1)]

    def xfm(c):
        return s_f[:, S * c: S * (c + 1)]

    def ofm(c):
        return s_b[:, Q0 + S * c: Q0 + S * (c + 1)]

    def acc_view(i):
        return pS[:, 512 * i:512 * (i + 1)]

    def psb(tensor, lo, hi):
        base = 0 if tensor is pS else 4
        return ["ps%d" % b for b in
                range(base + lo // 512, base + (hi - 1) // 512 + 1)]

    ALLXNT = ["xnT%d" % j for j in range(NJ)]
    ALLYNT = ["ynT%d" % j for j in range(NJ)]
    ALLOH = ["o_h%d_%d" % (h, b) for h in range(NH) for b in range(2)]

    XQ = ("qx", "qx2", "qx3", "qx4")

    def x_chunk_dma(j):
        src = dX[128 * j:128 * (j + 1), :]
        dst = s_x[:, 1024 * j:1024 * (j + 1)]
        op("pool", lambda e, src=src, dst=dst: e.dma_start(dst, src),
           writes=["x%d" % j], queue=XQ[j % 4])

    def x_dma(t):
        x_chunk_dma(2 * t)
        x_chunk_dma(2 * t + 1)

    pre_dma = [None]

    def w_chunk_dma_pre():
        src0 = dWq.rearrange("(kc p) n -> p kc n", p=128)[:, :, 0:128]
        dstv = s_wb[:, 0:1024].rearrange("p (kc n) -> p kc n", n=128)
        op("pool", lambda e, src0=src0, dstv=dstv: e.dma_start(dstv, src0),
           writes=["wb0"], queue="qpool")

    w_chunk_dma_pre()
    x_dma(0)
    x_dma(1)
    op("act", lambda e: e.dma_start(s_cst[:], dCst), writes=["cst"], queue="qact")
    op("act", lambda e: e.dma_start(s_bq[:], dBq), writes=["bq"], queue="qact")
    op("act", lambda e: e.dma_start(s_bk[:], dBk), writes=["bk"], queue="qact")
    op("act", lambda e: e.dma_start(s_bv[:], dBv), writes=["bv"], queue="qact")
    op("act", lambda e: e.dma_start(s_bo[:], dBo), writes=["bo"], queue="qact")

    def emit_ln(src_ap, src_res, dst_ap, dst_res, pid=0, eng="dve",
                dst_war=(), war0=(), m_cell=None, m_res=None, m_out=None):
        b = 16 * pid

        def cell(i):
            return s_st[:, b + i:b + i + 1]

        r = lambda n: "st%d_%s" % (pid, n)
        dmp = s_dmp[:, 0:1024]
        if m_cell is None:
            if m_out is not None:
                mc, mres = m_out
            else:
                mc, mres = cell(2), r("m")
            op("dve", lambda e, mc=mc: e.tensor_scalar(
                dmp, src_ap, 1.0 / D, 0.0, ALU.mult, ALU.add,
                accum_out=mc),
               reads=[src_res], writes=[mres, "dmp"], war=war0)
        else:
            mc, mres = m_cell, m_res
        op("dve", lambda e: e.scalar_tensor_tensor(
            dmp, src_ap, 1.0 / D, src_ap, ALU.mult, ALU.mult,
            accum_out=cell(3)),
           reads=[src_res], writes=[r("e2"), "dmp"],
           war=(war0 if m_cell is not None else ()))
        op("dve", lambda e, mc=mc: e.tensor_scalar(cell(4), mc, mc, -EPS,
                                                   ALU.mult, ALU.add),
           reads=[mres], writes=[r("v")])
        op("dve", lambda e: e.tensor_tensor(s_st[:, b + 4:b + 5], cell(3),
                                            cell(4), ALU.subtract),
           reads=[r("e2"), r("v")], writes=[r("v")])
        op("act", lambda e: e.activation(cell(5), cell(4), AF.Ln,
                                         bias=0.0, scale=1.0),
           reads=[r("v")], writes=[r("l")])
        op("act", lambda e: e.activation(cell(5), cell(5), AF.Exp,
                                         bias=0.0, scale=-0.5),
           reads=[r("l")], writes=[r("rs")])
        op("dve", lambda e, mc=mc: e.tensor_scalar(cell(6), mc, cell(5), -1.0,
                                                   ALU.mult, ALU.mult),
           reads=[mres, r("rs")], writes=[r("n")])
        op("act", lambda e: e.activation(dst_ap, src_ap, AF.Identity,
                                         bias=cell(6), scale=cell(5)),
           reads=[src_res, r("rs"), r("n")], writes=[dst_res], war=dst_war)

    def p1_L(j):
        par = j % 2
        xs = s_x[:, D * j:D * (j + 1)]
        xn = s_ln[:, 1024 * par:1024 * (par + 1)]
        emit_ln(xs, "x%d" % j, xn, "lnr%d" % par, pid=j % 4, eng="dve",
                m_out=(s_sx[:, j:j + 1], "mx%d" % j))

    def p1_T(j):
        par = j % 2
        xn = s_ln[:, 1024 * par:1024 * (par + 1)]
        pb = pO[:, 512 * par:512 * (par + 1)].bitcast(BF16)
        bank = ["ps%d" % (4 + par)]
        for c in range(ND):
            op("pe",
               lambda e, c=c, xn=xn, pb=pb: e.transpose(
                   pb[:, 128 * c:128 * (c + 1)],
                   xn[:, 128 * c:128 * (c + 1)], IDT),
               reads=["lnr%d" % par, "cst"],
               writes=(bank if c == 0 else []),
               bump=(bank if c > 0 else []))
        dst = s_f[:].rearrange("p (c n) -> p c n", c=ND)[:, :, 128 * j:128 * (j + 1)]
        src = pb[:, 0:1024].rearrange("p (c n) -> p c n", c=ND)
        op("act", lambda e, dst=dst, src=src: e.copy(dst, src),
           reads=bank, writes=["xnT%d" % j])


    def wslot(slot):
        if slot < 4:
            return s_wb[:, 1024 * slot:1024 * (slot + 1)]
        return s_wd[:, 1024 * (slot - 4):1024 * (slot - 3)]

    def w_chunk_dma(dram, c, slot, queue="qpool"):
        src = dram.rearrange("(kc p) n -> p kc n", p=128)[:, :, 128 * c:128 * (c + 1)]
        dstv = wslot(slot).rearrange("p (kc n) -> p kc n", n=128)
        op("pool", lambda e, src=src, dstv=dstv: e.dma_start(dstv, src),
           writes=["wb%d" % slot], queue=queue)

    def fm_matmuls(slot, wres_list, reads_fn, out_view_fn, psum_res_fn,
                   src_fn=None):
        if src_fn is None:
            src_fn = xfm
        for si in range(4):
            for kc in range(ND):
                first = kc == 0
                op("pe",
                   lambda e, kc=kc, si=si, slot=slot, first=first:
                   e.matmul(out_view_fn(si),
                            wslot(slot)[:, 128 * kc:128 * (kc + 1)],
                            src_fn(kc)[:, 512 * si:512 * (si + 1)],
                            start=first, stop=(kc == ND - 1)),
                   reads=wres_list + reads_fn(si),
                   writes=([psum_res_fn(si)] if first else []),
                   bump=([] if first else [psum_res_fn(si)]))

    norm_pend = []

    def norm_sq(which, c, si, src_ap, dres):
        par = q2par[0]
        q2par[0] ^= 1
        q2 = s_wd[:, 2048 + 512 * par:2048 + 512 * (par + 1)]
        op("pool", lambda e, src_ap=src_ap, q2=q2: e.tensor_mul(
            q2, src_ap, src_ap),
           reads=[dres], writes=["q2%d" % par])
        bcol = (0 if which == "q" else 32) + 4 * c + si
        norm_pend.append((par, bcol))

    def norm_flush(n=1):
        for _ in range(n):
            if not norm_pend:
                return
            par, bcol = norm_pend.pop(0)
            nrm = pS[64 * par:64 * par + 1, 512:1024]
            q2 = s_wd[:, 2048 + 512 * par:2048 + 512 * (par + 1)]
            op("pe", lambda e, q2=q2, nrm=nrm, par=par: e.matmul(
                nrm, ONES[:, 0:1], q2, start=True, stop=True,
                tile_position=(0, 64 * par), skip_group_check=True),
               reads=["q2%d" % par, "cst"], writes=["ps1"])
            op("dve", lambda e, bcol=bcol, nrm=nrm: e.reduce_max(
                s_Bw[0:1, bcol:bcol + 1], nrm, axis=AX.X),
               reads=["ps1"], writes=["Bw%d" % bcol])

    ubank = [0]
    q2par = [0]

    fifo = []

    def fifo_step(k=1):
        while fifo and k > 0:
            fifo.pop(0)()
            k -= 1

    def fifo_drain():
        while fifo:
            fifo.pop(0)()

    def push_qk_unit(which, c, si):
        bias_t, bias_res, dest_fn = (
            (s_bq, "bq", q_fm) if which == "q" else (s_bk, "bk", k_fm))
        slot = (3 * c + (0 if which == "q" else 1)) % 6
        out = pS[:, 0:512]

        def piece(kc0):
            if kc0 == 0 and len(norm_pend) >= 2:
                norm_flush(1)
            for kc in (kc0, kc0 + 1):
                first = kc == 0
                op("pe",
                   lambda e, kc=kc, si=si, slot=slot, first=first, out=out:
                   e.matmul(out, wslot(slot)[:, 128 * kc:128 * (kc + 1)],
                            xfm(kc)[:, 512 * si:512 * (si + 1)],
                            start=first, stop=(kc == ND - 1),
                            skip_group_check=True),
                   reads=["wb%d" % slot] + ["xnT%d" % (4 * si + k)
                                            for k in range(4)],
                   writes=(["ps0"] if first else []),
                   bump=([] if first else ["ps0"]))
            if kc0 == ND - 2:
                dres = "%s_%d_%d" % (which, c, si)
                dst = dest_fn(c)[:, 512 * si:512 * (si + 1)]
                op("dve",
                   lambda e, dst=dst, out=out, bias_t=bias_t, c=c:
                   e.tensor_scalar(dst, out, bias_t[:, c:c + 1], None,
                                   ALU.add),
                   reads=["ps0", bias_res], writes=[dres])
                norm_sq(which, c, si, dst, dres)
        for kc0 in range(0, ND, 2):
            fifo.append(lambda kc0=kc0: piece(kc0))

    def emit_qk_unit(which, c, si):
        push_qk_unit(which, c, si)
        fifo_drain()

    def emit_B_calc(c):
        op("dve", lambda e, c=c: e.reduce_max(
            s_Bw[0:1, 64 + c:65 + c], s_Bw[0:1, 4 * c:4 * c + 4], axis=AX.X),
           reads=["Bw%d" % (4 * c + i) for i in range(4)], writes=["qm%d" % c])
        op("dve", lambda e, c=c: e.reduce_max(
            s_Bw[0:1, 72 + c:73 + c], s_Bw[0:1, 32 + 4 * c:36 + 4 * c],
            axis=AX.X),
           reads=["Bw%d" % (32 + 4 * c + i) for i in range(4)],
           writes=["km%d" % c])
        op("dve", lambda e, c=c: e.tensor_tensor(
            s_Bw[0:1, 80 + c:81 + c], s_Bw[0:1, 64 + c:65 + c],
            s_Bw[0:1, 72 + c:73 + c], ALU.mult),
           reads=["qm%d" % c, "km%d" % c], writes=["Bqk%d" % c])
        op("act", lambda e, c=c: e.activation(
            s_Bw[0:1, 88 + c:89 + c], s_Bw[0:1, 80 + c:81 + c], AF.Ln,
            bias=0.0, scale=1.0),
           reads=["Bqk%d" % c], writes=["Bln%d" % c])
        op("act", lambda e, c=c: e.activation(
            s_Bw[0:1, 96 + c:97 + c], s_Bw[0:1, 88 + c:89 + c], AF.Exp,
            bias=0.0, scale=0.5),
           reads=["Bln%d" % c], writes=["Bex%d" % c])
        nbrow = s_wd[0:1, 3072 + c:3073 + c]
        op("dve", lambda e, c=c, nbrow=nbrow: e.tensor_scalar(
            nbrow, s_Bw[0:1, 96 + c:97 + c], -0.125, None, ALU.mult),
           reads=["Bex%d" % c], writes=["nbrow%d" % c])

    def emit_B_nbp(c):
        nbrow = s_wd[0:1, 3072 + c:3073 + c]
        nbp = pS[:, 0:1]
        i1 = op("pe", lambda e, nbrow=nbrow, nbp=nbp: e.matmul(
            nbp[0:64, :], ONES[0:1, 0:64], nbrow, start=True, stop=True,
            skip_group_check=True),
            reads=["nbrow%d" % c, "cst"], writes=["ps0"])
        op("pe", lambda e, nbrow=nbrow, nbp=nbp: e.matmul(
            nbp[64:128, :], ONES[0:1, 0:64], nbrow, start=True, stop=True,
            tile_position=(0, 64), skip_group_check=True),
           reads=["nbrow%d" % c, "cst"], deps=[i1], bump=["ps0"])
        op("dve", lambda e, c=c, nbp=nbp: e.tensor_copy(
            s_nB[:, c:c + 1], nbp),
           reads=["ps0"], writes=["nB%d" % c])

    def emit_v_round(c, si):
        slot = (3 * c + 2) % 6
        bank = ubank[0]
        ubank[0] ^= 1
        bres = "ps%d" % bank
        out = pS[:, 512 * bank:512 * (bank + 1)]
        for kc in range(ND):
            first = kc == 0
            op("pe",
               lambda e, kc=kc, si=si, slot=slot, first=first, out=out:
               e.matmul(out, wslot(slot)[:, 128 * kc:128 * (kc + 1)],
                        xfm(kc)[:, 512 * si:512 * (si + 1)],
                        start=first, stop=(kc == ND - 1)),
               reads=["wb%d" % slot] + ["xnT%d" % (4 * si + k) for k in range(4)],
               writes=([bres] if first else []),
               bump=([] if first else [bres]))
        vstage = s_at[:, 2048 * (c % 2):2048 * (c % 2) + 2048]
        atres = "at%d" % (c % 2)
        op("dve",
           lambda e, si=si, c=c, vstage=vstage, out=out: e.tensor_scalar(
               vstage[:, 512 * si:512 * (si + 1)], out,
               s_bv[:, c:c + 1], None, ALU.add),
           reads=[bres, "bv"],
           writes=([atres] if si == 0 else []),
           bump=([] if si == 0 else [atres]))
        pb = out.bitcast(BF16)
        for jj in range(4):
            j = 4 * si + jj
            op("pe",
               lambda e, j=j, jj=jj, pb=pb, vstage=vstage: e.transpose(
                   pb[:, 128 * jj:128 * (jj + 1)],
                   vstage[:, 128 * j:128 * (j + 1)], IDT),
               reads=[atres, "cst"],
               writes=([bres] if jj == 0 else []),
               bump=([] if jj == 0 else [bres]))
        for jj in range(4):
            j = 4 * si + jj
            for hh in range(2):
                h = 2 * c + hh
                dst = s_b[:, V0 + 1040 * j + 65 * h:
                          V0 + 1040 * j + 65 * h + 64]
                op("dve",
                   lambda e, dst=dst, hh=hh, jj=jj, pb=pb: e.tensor_copy(
                       dst, pb[:, 128 * jj + 64 * hh:128 * jj + 64 * hh + 64]),
                   reads=[bres, "vones"],
                   writes=(["vx_%d_%d" % (j, c)] if hh == 0 else []),
                   bump=([] if hh == 0 else ["vx_%d_%d" % (j, c)]))

    op("pool", lambda e: e.memset(s_b[:, V0:V0 + 1040 * NJ + 128], 1.0),
       writes=["vones"])

    def emit_norm_half(h, hf, part):
        c = h // 2
        lo = 1024 * hf
        bks = ["ps%d" % (4 + 2 * hf), "ps%d" % (5 + 2 * hf)]
        o_res = "o_h%d_%d" % (h, hf)
        osc = s_ln[0:64, lo:lo + 1024]
        rr = s_dmp[64:65, lo:lo + 1024]
        if part == 0:
            op("dve", lambda e, osc=osc, lo=lo: e.tensor_copy(
                osc, pO[0:64, lo:lo + 1024]),
               reads=bks, writes=["osc%d" % hf],
               war=(["lnr0", "lnr1"] if h == 0 else ()))
            op("act", lambda e, lo=lo: e.activation(
                pO[64:65, lo:lo + 1024], pO[64:65, lo:lo + 1024], AF.Ln,
                bias=0.0, scale=1.0),
               reads=[bks[0]], writes=bks)
            op("act", lambda e, rr=rr, lo=lo: e.activation(
                rr, pO[64:65, lo:lo + 1024], AF.Exp, bias=0.0, scale=-1.0),
               reads=bks, writes=["rrq%d" % hf], bump=["dmp"],
               war=(["dmp"] if h == 0 else ()))
        else:
            i1 = op("pe", lambda e, rr=rr, lo=lo: e.matmul(
                pO[0:64, lo:lo + 512], s_cst[64:65, 384:448], rr[:, 0:512],
                start=True, stop=True, skip_group_check=True),
                reads=["rrq%d" % hf, "osc%d" % hf], writes=[bks[0]])
            op("pe", lambda e, rr=rr, lo=lo: e.matmul(
                pO[0:64, lo + 512:lo + 1024], s_cst[64:65, 384:448],
                rr[:, 512:1024],
                start=True, stop=True, skip_group_check=True),
                reads=["rrq%d" % hf, "osc%d" % hf], writes=[bks[1]])
            qres2 = ["q_%d_%d" % (c, 2 * hf), "q_%d_%d" % (c, 2 * hf + 1)]
            if h % 2 == 0:
                dst = ofm(c)[0:64, lo:lo + 1024]
                op("dve", lambda e, dst=dst, osc=osc, lo=lo: e.tensor_tensor(
                    dst, osc, pO[0:64, lo:lo + 1024], ALU.mult),
                   reads=["osc%d" % hf] + bks, writes=[o_res],
                   war=qres2, bump=qres2)
            else:
                onorm = s_dmp[0:64, lo:lo + 1024]
                op("dve", lambda e, onorm=onorm, osc=osc, lo=lo:
                   e.tensor_tensor(onorm, osc, pO[0:64, lo:lo + 1024],
                                   ALU.mult),
                   reads=["osc%d" % hf] + bks, writes=["onq%d" % hf],
                   bump=["dmp"],
                   war=(["dmp"] if h == 1 else ()))
                op("sp", lambda e, c=c, onorm=onorm, lo=lo: e.dma_start(
                    ofm(c)[64:128, lo:lo + 1024], onorm),
                   reads=["onq%d" % hf], writes=[o_res], queue="qsp",
                   war=qres2)

    kpadv = [s_y2[:, 0:1024].bitcast(BF16), s_y2[:, 1024:2048].bitcast(BF16)]
    op("dve", lambda e: e.memset(kpadv[0][64:128, :], 0.0), writes=["kpz0"])
    op("dve", lambda e: e.memset(kpadv[1][0:64, :], 0.0), writes=["kpz1"])
    pend_av = []
    pend_n1 = [None]
    sctr = [1]

    def emit_av_chunk(ph, pk, q0, n):
        c = ph // 2
        par = pk % 2
        atres = "at%d" % par
        first = pk == 0
        lastk = min((q0 + n - 1) // 128, NJ - 1)
        banks = psb(pO, q0, q0 + n)
        op("pe",
           lambda e, j=pk, h=ph, q0=q0, n=n, par=par, first=first,
           stopf=(pk == lastk):
           e.matmul(pO[:, q0:q0 + n],
                    s_b[:, V0 + 1040 * j + 65 * h:
                        V0 + 1040 * j + 65 * h + 128],
                    s_at[:, 2048 * par + (q0 - 128 * j):
                         2048 * par + (q0 - 128 * j) + n],
                    start=first, stop=stopf, skip_group_check=True),
           reads=[atres, "vx_%d_%d" % (pk, c), "vones"],
           writes=(banks if first else []),
           bump=([] if first else banks))

    def emit_pend_batch(pv):
        ph, pk, pch = pv
        for (q0, n) in pch:
            emit_av_chunk(ph, pk, q0, n)

        def post():
            if pk == NJ - 1:
                emit_norm_half(ph, 1, 0)
                pend_n1[0] = ph
            elif pk == 7:
                emit_norm_half(ph, 0, 0)
                emit_norm_half(ph, 0, 1)
        return post

    def emit_attn_step(h, ktc):
        c = h // 2
        hp = h % 2
        if ktc == 8 and 1 <= h < NH - 1:
            emit_kpad_copy(h + 1)
        post = None
        if len(pend_av) == 2:
            if pend_av[0][1] == 0 and pend_n1[0] is not None:
                emit_norm_half(pend_n1[0], 1, 1)
                pend_n1[0] = None
            post = emit_pend_batch(pend_av.pop(0))
        defer_n1 = pend_n1[0] is not None
        qres = ["q_%d_%d" % (c, si) for si in range(4)]
        par = ktc % 2
        atres = "at%d" % par
        chunks = []
        q0 = 128 * ktc
        while q0 < S:
            n = min(512 - (q0 % 512), S - q0)
            chunks.append((q0, n))
            q0 += n
        first_exp = True
        gi = 0
        while gi < len(chunks):
            lo = 0 if not fifo else 1
            s = sctr[0]
            if s < lo:
                s = lo
            g = 2 if (gi + 1 < len(chunks)
                      and chunks[gi + 1][1] == chunks[gi][1]
                      and s != 3) else 1
            sctr[0] = lo + (s + g - lo) % (4 - lo)
            for u in range(g):
                q0, n = chunks[gi + u]
                sl = s + u
                diag = gi + u == 0
                op("pe",
                   lambda e, sl=sl, q0=q0, n=n, ktc=ktc, hp=hp, c=c,
                   diag=diag:
                   e.matmul(pS[:, 512 * sl:512 * sl + n],
                            kpadv[hp][:, 128 * ktc:128 * (ktc + 1)],
                            q_fm(c)[:, q0:q0 + n],
                            start=True, stop=(not diag),
                            skip_group_check=True),
                   reads=["kp%d" % hp] + qres, writes=["ps%d" % sl])
                if diag:
                    op("pe",
                       lambda e, sl=sl: e.matmul(
                           pS[:, 512 * sl:512 * sl + 128], NEGI, LMSK,
                           start=False, stop=True, skip_group_check=True),
                       reads=["cst"], bump=["ps%d" % sl])
            q0, n = chunks[gi]
            dof = 2048 * par + (q0 - 128 * ktc)
            if g == 2:
                src = pS[:, 512 * s:512 * (s + 2)].rearrange(
                    "p (g n) -> p g n", g=2)[:, :, 0:n]
                dst = s_at[:, dof:dof + 2 * n].rearrange(
                    "p (g n) -> p g n", g=2)
            else:
                src = pS[:, 512 * s:512 * s + n]
                dst = s_at[:, dof:dof + n]
            op("act",
               lambda e, src=src, dst=dst, c=c: e.activation(
                   dst, src, AF.Exp, bias=s_nB[:, c:c + 1], scale=QSCALE),
               reads=["nB%d" % c] + ["ps%d" % (s + k) for k in range(g)],
               writes=([atres] if first_exp else []),
               bump=([] if first_exp else [atres]))
            first_exp = False
            fifo_step(3 if defer_n1 else 1)
            gi += g
        if defer_n1:
            emit_norm_half(pend_n1[0], 1, 1)
            pend_n1[0] = None
        if post is not None:
            post()
        pend_av.append((h, ktc, chunks))

    def emit_kpad_copy(h):
        c = h // 2
        rbase = 64 * (h % 2)
        hp = h % 2
        kres = ["k_%d_%d" % (c, si) for si in range(4)]
        op("dve",
           lambda e, hp=hp, c=c, rbase=rbase: e.tensor_copy(
               kpadv[hp][rbase:rbase + 64, :], k_fm(c)[rbase:rbase + 64, :]),
           reads=kres + ["kpz%d" % hp], writes=["kp%d" % hp])

    def attention_drain():
        while pend_av:
            if pend_n1[0] is not None:
                emit_norm_half(pend_n1[0], 1, 1)
                pend_n1[0] = None
            emit_pend_batch(pend_av.pop(0))()

    for t in range(NJ + 1):
        if t < NJ:
            if t % 2 == 0 and t // 2 + 2 < NJ // 2:
                x_dma(t // 2 + 2)
            p1_L(t)
            if t == 0:
                w_chunk_dma(dWk, 0, 1)
                w_chunk_dma(dWv, 0, 2)
            elif t == 2:
                w_chunk_dma(dWq, 1, 3)
            elif t == 4:
                w_chunk_dma(dWk, 1, 4)
            elif t == 6:
                w_chunk_dma(dWv, 1, 5)
        if t >= 1:
            j = t - 1
            p1_T(j)
            if j % 4 == 3:
                si = j // 4
                emit_qk_unit("q", 0, si)
                emit_qk_unit("k", 0, si)
                emit_v_round(0, si)
    norm_flush(99)
    emit_B_calc(0)
    emit_B_nbp(0)
    emit_kpad_copy(0)
    emit_kpad_copy(1)
    w_chunk_dma(dWq, 2, 0)
    w_chunk_dma(dWk, 2, 1)
    w_chunk_dma(dWv, 2, 2)
    if trunc_step is not None and trunc_step <= 0:
        P.emit_all(final_deps=[])
        return nc

    for c in range(1, ND):
        steps = [(h, ktc) for h in (2 * c - 2, 2 * c - 1) for ktc in range(NJ)]
        for w, si in [(w, si) for si in range(4) for w in ("q", "k")]:
            push_qk_unit(w, c, si)
        fifo.append(lambda c=c: (norm_flush(99), emit_B_calc(c)))
        fifo.append(lambda c=c: emit_B_nbp(c))
        for h, ktc in steps:
            emit_attn_step(h, ktc)
        fifo_drain()
        attention_drain()
        for si in range(4):
            emit_v_round(c, si)
        if c + 2 < ND:
            w_chunk_dma(dWq, c + 2, (3 * (c + 2)) % 6)
            w_chunk_dma(dWk, c + 2, (3 * (c + 2) + 1) % 6)
            w_chunk_dma(dWv, c + 2, (3 * (c + 2) + 2) % 6)
        if trunc_step is not None and c >= trunc_step:
            P.emit_all(final_deps=[])
            return nc
    for h in (NH - 2, NH - 1):
        for ktc in range(NJ):
            emit_attn_step(h, ktc)
    attention_drain()
    emit_norm_half(NH - 1, 1, 1)
    pend_n1[0] = None
    if trunc:
        P.emit_all(final_deps=[])
        return nc

    for c in range(ND):
        slot = c % 2
        w_chunk_dma(dWo, c, slot)
        fm_matmuls(slot, ["wb%d" % slot],
                   lambda si: ["o_h%d_%d" % (h2, si // 2) for h2 in range(NH)],
                   acc_view, lambda si: "ps%d" % si, src_fn=ofm)
        for si in range(4):
            op("act",
               lambda e, c=c, si=si: e.activation(
                   s_f[:, S * c + 512 * si: S * c + 512 * (si + 1)],
                   acc_view(si), AF.Identity, bias=s_bo[:, c:c + 1], scale=1.0),
               reads=["ps%d" % si, "bo"], writes=["zf_%d_%d" % (c, si)],
               war=["xnT%d" % (4 * si + k) for k in range(4)])

    def p3_A(j):
        par = j % 2
        bank1 = "ps%d" % (4 + par)
        pb = pO[:, 512 * par:512 * (par + 1)].bitcast(BF16)
        for c in range(ND):
            op("pe",
               lambda e, c=c, j=j, pb=pb: e.transpose(
                   pb[:, 128 * c:128 * (c + 1)],
                   s_f[:, S * c + 128 * j: S * c + 128 * (j + 1)], IDT),
               reads=["zf_%d_%d" % (c, j // 4), "cst"],
               writes=([bank1] if c == 0 else []),
               bump=([] if c == 0 else [bank1]))
        ztm = s_ln[:, 1024 * par:1024 * (par + 1)]
        op("dve", lambda e, pb=pb, ztm=ztm: e.tensor_copy(ztm, pb[:, 0:1024]),
           reads=[bank1], writes=["lnr%d" % par],
           war=["osc0", "osc1"])
        yn = s_at[:, 1024 * par:1024 * (par + 1)]
        emit_ln(ztm, "lnr%d" % par, yn, "yn%d" % par, pid=j % 4, eng="dve",
                dst_war=("at0",) if j < 2 else (),
                war0=(["rrq0", "rrq1", "onq0", "onq1"] if j == 0 else ()))

    def p3_B(j):
        par = j % 2
        yn = s_at[:, 1024 * par:1024 * (par + 1)]
        op("pool",
           lambda e, j=j, yn=yn: e.tensor_tensor(
               s_x[:, D * j:D * (j + 1)], s_x[:, D * j:D * (j + 1)], yn,
               ALU.add),
           reads=["yn%d" % par, "x%d" % j], writes=["x%d" % j])

    def p3_C(j):
        par = j % 2
        yn2 = s_at[:, 2048 + 1024 * par:2048 + 1024 * (par + 1)]
        emit_ln(s_x[:, D * j:D * (j + 1)], "x%d" % j, yn2, "yn2%d" % par,
                pid=4 + j % 4, eng="dve",
                m_cell=s_sx[:, j:j + 1], m_res="mx%d" % j,
                dst_war=("at1",) if j < 2 else ())

    def p3_D(j):
        par = j % 2
        yn2 = s_at[:, 2048 + 1024 * par:2048 + 1024 * (par + 1)]
        bank2 = "ps%d" % (6 + par)
        pb2 = pO[:, 1024 + 512 * par:1536 + 512 * par].bitcast(BF16)
        for c in range(ND):
            op("pe",
               lambda e, c=c, pb2=pb2, yn2=yn2: e.transpose(
                   pb2[:, 128 * c:128 * (c + 1)],
                   yn2[:, 128 * c:128 * (c + 1)], IDT),
               reads=["yn2%d" % par, "cst"],
               writes=([bank2] if c == 0 else []),
               bump=([] if c == 0 else [bank2]))
        dst = s_f[:].rearrange("p (c n) -> p c n", c=ND)[:, :, 128 * j:128 * (j + 1)]
        src = pb2[:, 0:1024].rearrange("p (c n) -> p c n", c=ND)
        op("act", lambda e, dst=dst, src=src: e.copy(dst, src),
           reads=[bank2], writes=["ynT%d" % j],
           war=["zf_%d_%d" % (c2, j // 4) for c2 in range(ND)])

    for t in range(NJ + 3):
        if t < NJ:
            p3_A(t)
        if 0 <= t - 1 < NJ:
            p3_B(t - 1)
        if 0 <= t - 2 < NJ:
            p3_C(t - 2)
        if 0 <= t - 3 < NJ:
            p3_D(t - 3)

    for hc in range(NHC):
        gslot = hc % 2
        uslot = 2 + hc % 2
        w_chunk_dma(dWg, hc, gslot)
        w_chunk_dma(dWu, hc, uslot)
        hwar = []
        if hc < ND:
            hwar = ["zf_%d_%d" % (hc, s2) for s2 in range(4)]
        elif hc < 2 * ND:
            hwar = ["k_%d_%d" % (hc - ND, s2) for s2 in range(4)]
        elif hc == 2 * ND:
            hwar = (["vx_%d_%d" % (j2, c2) for j2 in range(NJ)
                     for c2 in range(ND)] + ["vones"])
        for si in range(4):
            for kc in range(ND):
                first = kc == 0
                op("pe",
                   lambda e, kc=kc, si=si, gslot=gslot, first=first:
                   e.matmul(acc_view(si),
                            wslot(gslot)[:, 128 * kc:128 * (kc + 1)],
                            xfm(kc)[:, 512 * si:512 * (si + 1)],
                            start=first, stop=(kc == ND - 1)),
                   reads=["wb%d" % gslot] + ["ynT%d" % (4 * si + k) for k in range(4)],
                   writes=(["ps%d" % si] if first else []),
                   bump=([] if first else ["ps%d" % si]))
            for kc in range(ND):
                first = kc == 0
                op("pe",
                   lambda e, kc=kc, si=si, uslot=uslot, first=first:
                   e.matmul(pO[:, 512 * si:512 * (si + 1)],
                            wslot(uslot)[:, 128 * kc:128 * (kc + 1)],
                            xfm(kc)[:, 512 * si:512 * (si + 1)],
                            start=first, stop=(kc == ND - 1)),
                   reads=["wb%d" % uslot] + ["ynT%d" % (4 * si + k) for k in range(4)],
                   writes=(["ps%d" % (4 + si)] if first else []),
                   bump=([] if first else ["ps%d" % (4 + si)]))
            stg = s_dmp[:, 1024 + 512 * (si % 2): 1536 + 512 * (si % 2)]
            op("act",
               lambda e, si=si, stg=stg: e.activation(
                   stg, acc_view(si), AF.Silu, bias=0.0, scale=1.0 / 127.0),
               reads=["ps%d" % si], writes=["silu%d" % (si % 2)],
               war=(("rrq1", "onq1") if (hc == 0 and si < 2) else ()))
            op("dve",
               lambda e, hc=hc, si=si, stg=stg: e.tensor_tensor(
                   s_b[:, S * hc + 512 * si: S * hc + 512 * (si + 1)],
                   stg, pO[:, 512 * si:512 * (si + 1)], ALU.mult),
               reads=["silu%d" % (si % 2), "ps%d" % (4 + si)],
               writes=["h_%d_%d" % (hc, si)],
               war=hwar if si == 0 else ())

    def wd_dma(c):
        slot = c % 2
        src = dWd.rearrange("(hc p) n -> p hc n", p=128)[:, :, 128 * c:128 * (c + 1)]
        dstv = s_wd[:, 3072 * slot:3072 * (slot + 1)].rearrange(
            "p (hc n) -> p hc n", n=128)
        op("pool", lambda e, src=src, dstv=dstv: e.dma_start(dstv, src),
           writes=["wd%d_0" % slot, "wd%d_1" % slot], queue="qx",
           war=((["wb4", "wb5", "q20", "q21"]
                 + ["nbrow%d" % c2 for c2 in range(ND)]) if c < 2 else ()))

    wd_dma(0)
    wd_dma(1)
    for c in range(ND):
        slot = c % 2
        for si in range(4):
            for hc in range(NHC):
                first = hc == 0
                op("pe",
                   lambda e, hc=hc, si=si, slot=slot, first=first:
                   e.matmul(acc_view(si),
                            s_wd[:, 3072 * slot + 128 * hc:
                                 3072 * slot + 128 * (hc + 1)],
                            s_b[:, S * hc + 512 * si: S * hc + 512 * (si + 1)],
                            start=first, stop=(hc == NHC - 1)),
                   reads=["wd%d_%d" % (slot, hc // 12), "h_%d_%d" % (hc, si)],
                   writes=(["ps%d" % si] if first else []),
                   bump=([] if first else ["ps%d" % si]))
            op("act",
               lambda e, c=c, si=si: e.activation(
                   s_f[:, S * c + 512 * si: S * c + 512 * (si + 1)],
                   acc_view(si), AF.Identity, bias=0.0, scale=FFS),
               reads=["ps%d" % si], writes=["df_%d_%d" % (c, si)],
               war=(ALLYNT if (c == 0 and si == 0) else ()))
        if c + 2 < ND:
            wd_dma(c + 2)

    oY = []

    def p6_T(j):
        par = j % 2
        bank = "ps%d" % (4 + par)
        pb = pO[:, 512 * par:512 * (par + 1)].bitcast(BF16)
        for c in range(ND):
            op("pe",
               lambda e, c=c, j=j, pb=pb: e.transpose(
                   pb[:, 128 * c:128 * (c + 1)],
                   s_f[:, S * c + 128 * j: S * c + 128 * (j + 1)], IDT),
               reads=["df_%d_%d" % (c, j // 4), "cst"],
               writes=([bank] if c == 0 else []),
               bump=([] if c == 0 else [bank]))
        dtm = s_ln[:, 1024 * par:1024 * (par + 1)]
        op("act", lambda e, pb=pb, dtm=dtm: e.copy(dtm, pb[:, 0:1024]),
           reads=[bank], writes=["lnr%d" % par])

    def p6_L(j):
        par = j % 2
        dtm = s_ln[:, 1024 * par:1024 * (par + 1)]
        y2 = s_y2[:, 1024 * par:1024 * (par + 1)]
        emit_ln(dtm, "lnr%d" % par, y2, "y2_%d" % par, pid=j % 4, eng="dve",
                dst_war=("kp%d" % par, "kpz%d" % par) if j < 2 else ())

    def p6_AD(j):
        par = j % 2
        y2 = s_y2[:, 1024 * par:1024 * (par + 1)]
        y2r = "y2_%d" % par
        if j % 2:
            op("pool",
               lambda e, j=j, y2=y2: e.tensor_tensor(
                   y2, y2, s_x[:, D * j:D * (j + 1)], ALU.add),
               reads=[y2r, "x%d" % j], writes=[y2r])
        else:
            op("dve",
               lambda e, j=j, y2=y2: e.tensor_tensor(
                   y2, y2, s_x[:, D * j:D * (j + 1)], ALU.add),
               reads=[y2r, "x%d" % j], writes=[y2r])
        yq = ("qsp", "qsp2", "qsp3")[j % 3]
        oY.append(op("sp",
                     lambda e, j=j, y2=y2: e.dma_start(
                         dY[128 * j:128 * (j + 1), :], y2),
                     reads=[y2r], writes=["yout%d" % j], queue=yq))

    for t in range(NJ + 2):
        if t < NJ:
            p6_T(t)
        if 0 <= t - 1 < NJ:
            p6_L(t - 1)
        if 0 <= t - 2 < NJ:
            p6_AD(t - 2)

    P.emit_all(final_deps=oY)
    return nc


def kernel(L0_attn_X, L0_attn_Wq, L0_attn_bq, L0_attn_Wk, L0_attn_bk,
           L0_attn_Wv, L0_attn_bv, L0_attn_Wo, L0_attn_bo,
           L0_ffn_w_gate, L0_ffn_w_up, L0_ffn_w_down):
    X = np.ascontiguousarray(np.asarray(L0_attn_X, dtype=np.float32))
    wq = np.ascontiguousarray(np.asarray(L0_attn_Wq, dtype=np.float32))
    wk = np.ascontiguousarray(np.asarray(L0_attn_Wk, dtype=np.float32))
    wv = np.ascontiguousarray(np.asarray(L0_attn_Wv, dtype=np.float32))
    wo = np.ascontiguousarray(np.asarray(L0_attn_Wo, dtype=np.float32))
    bqr = np.ascontiguousarray(np.asarray(L0_attn_bq, np.float32).reshape(ND, 128).T)
    bkr = np.ascontiguousarray(np.asarray(L0_attn_bk, np.float32).reshape(ND, 128).T)
    bvr = np.ascontiguousarray(np.asarray(L0_attn_bv, np.float32).reshape(ND, 128).T)
    bor = np.ascontiguousarray(np.asarray(L0_attn_bo, np.float32).reshape(ND, 128).T)
    wg = np.ascontiguousarray(np.asarray(L0_ffn_w_gate, np.int32).T)
    wu = np.ascontiguousarray(np.asarray(L0_ffn_w_up, np.int32).T)
    wd = np.ascontiguousarray(np.asarray(L0_ffn_w_down, np.int32).T)

    consts = np.zeros((128, 448), dtype=ml_dtypes.bfloat16)
    consts[:, 0:128] = np.eye(128, dtype=np.float32).astype(ml_dtypes.bfloat16)
    consts[:, 128:256] = np.tril(np.ones((128, 128), np.float32), -1).astype(
        ml_dtypes.bfloat16)
    consts[:, 256:384] = (LNEG * np.eye(128, dtype=np.float32)).astype(
        ml_dtypes.bfloat16)
    consts[:, 384:448] = np.ones((128, 64), np.float32).astype(ml_dtypes.bfloat16)

    nc = build_program()
    shared = {"wq": wq, "wk": wk, "wv": wv, "wo": wo,
              "bqr": bqr, "bkr": bkr, "bvr": bvr, "bor": bor,
              "wg": wg, "wu": wu, "wd": wd, "consts": consts}
    in_maps = [dict(shared, x=np.ascontiguousarray(X[i])) for i in range(8)]
    res = run_bass_kernel_spmd(nc, in_maps, core_ids=list(range(8)))
    return np.stack([np.asarray(res.results[i]["y"], dtype=np.float32)
                     for i in range(8)], axis=0)


# version 40
# speedup vs baseline: 1.0055x; 1.0003x over previous
import numpy as np
import ml_dtypes
import concourse.bass as bass
import concourse.mybir as mybir
from concourse.bass_utils import run_bass_kernel_spmd

F32 = mybir.dt.float32
BF16 = mybir.dt.bfloat16
I32 = mybir.dt.int32
AF = mybir.ActivationFunctionType
ALU = mybir.AluOpType
AX = mybir.AxisListType

S = 2048
D = 1024
NH = 16
FF = 3072
NJ = S // 128
ND = D // 128
NHC = FF // 128
EPS = 1e-5
QSCALE = 0.125
FFS = 1.0 / (127.0 * 127.0)
LNEG = -1.0e9


class Prog:

    ENGINES = ("pe", "act", "dve", "pool", "sp")

    def __init__(self, nc):
        self.nc = nc
        self.ops = []
        self.res = {}
        self.sems = {name: nc.alloc_semaphore("s_" + name)
                     for name in ("pe", "act", "dve", "pool", "qsp", "qact",
                                  "qpool", "qx", "qx2", "qx3", "qx4",
                                  "qsp2", "qsp3")}

    def op(self, engine, emit, reads=(), writes=(), queue=None, war=(),
           deps=(), bump=()):
        idx = len(self.ops)
        dep = set(deps)
        for r in reads:
            e = self.res.get(r)
            if e is not None and e[0] is not None:
                dep.add(e[0])
        for w in tuple(writes) + tuple(war):
            e = self.res.get(w)
            if e is not None:
                if e[0] is not None:
                    dep.add(e[0])
                dep |= e[1]
        dep.discard(idx)
        self.ops.append({"engine": engine, "emit": emit, "deps": dep,
                         "queue": queue, "token": None})
        for r in reads:
            self.res.setdefault(r, [None, set()])[1].add(idx)
        for w in writes:
            self.res[w] = [idx, set()]
        for w in bump:
            self.res.setdefault(w, [None, set()])[0] = idx
        return idx

    def emit_all(self, final_deps):
        ops = self.ops
        needed = set(final_deps)
        for o in ops:
            needed |= o["deps"]
        cnt = {k: 0 for k in self.sems}
        for i, o in enumerate(ops):
            ch = o["queue"] if o["queue"] else o["engine"]
            if o["queue"] is not None or i in needed:
                inc = 16 if o["queue"] else 1
                cnt[ch] += inc
                o["token"] = (ch, cnt[ch], inc)
        per_eng = {e: [] for e in self.ENGINES}
        for i, o in enumerate(ops):
            per_eng[o["engine"]].append(i)
        sems = self.sems
        burst_end_val = {}
        burst_settle = {}
        for i, o in enumerate(ops):
            if o["queue"] is not None:
                _, val, _ = o["token"]
                burst_end_val[i] = val
                burst_settle[i] = val - 16

        def body(ename, eng):
            wm = {}

            def wait(ch, val):
                if wm.get(ch, 0) < val:
                    eng.wait_ge(sems[ch], val)
                    wm[ch] = val

            for i in per_eng[ename]:
                o = ops[i]
                if o["queue"] is not None and i in burst_settle:
                    wait(o["queue"], burst_settle[i])
                for d in sorted(o["deps"]):
                    ch, val, _ = ops[d]["token"]
                    if ops[d]["queue"] is not None:
                        val = burst_end_val[d]
                    wait(ch, val)
                inst = o["emit"](eng)
                if o["token"] is not None:
                    ch, val, inc = o["token"]
                    inst.then_inc(sems[ch], inc)
            if ename == "sp":
                for d in sorted(final_deps):
                    ch, val, _ = ops[d]["token"]
                    wait(ch, val)

        with self.nc.Block() as blk:
            blk.tensor(lambda e: body("pe", e))
            blk.scalar(lambda e: body("act", e))
            blk.vector(lambda e: body("dve", e))
            blk.gpsimd(lambda e: body("pool", e))
            blk.sync(lambda e: body("sp", e))


def build_program(trunc=0, trunc_step=None):
    nc = bass.Bass("TRN2", target_bir_lowering=False)

    dX = nc.dram_tensor("x", [S, D], F32, kind="ExternalInput").ap()
    dWq = nc.dram_tensor("wq", [D, D], F32, kind="ExternalInput").ap()
    dWk = nc.dram_tensor("wk", [D, D], F32, kind="ExternalInput").ap()
    dWv = nc.dram_tensor("wv", [D, D], F32, kind="ExternalInput").ap()
    dWo = nc.dram_tensor("wo", [D, D], F32, kind="ExternalInput").ap()
    dBq = nc.dram_tensor("bqr", [128, ND], F32, kind="ExternalInput").ap()
    dBk = nc.dram_tensor("bkr", [128, ND], F32, kind="ExternalInput").ap()
    dBv = nc.dram_tensor("bvr", [128, ND], F32, kind="ExternalInput").ap()
    dBo = nc.dram_tensor("bor", [128, ND], F32, kind="ExternalInput").ap()
    dWg = nc.dram_tensor("wg", [D, FF], I32, kind="ExternalInput").ap()
    dWu = nc.dram_tensor("wu", [D, FF], I32, kind="ExternalInput").ap()
    dWd = nc.dram_tensor("wd", [FF, D], I32, kind="ExternalInput").ap()
    dCst = nc.dram_tensor("consts", [128, 448], BF16, kind="ExternalInput").ap()
    dY = nc.dram_tensor("y", [S, D], F32, kind="ExternalOutput").ap()

    s_x = nc.alloc_sbuf_tensor("s_x", [128, NJ * D], BF16)
    s_f = nc.alloc_sbuf_tensor("s_f", [128, ND * S], BF16)
    s_b = nc.alloc_sbuf_tensor("s_b", [128, 49536], BF16)
    s_at = nc.alloc_sbuf_tensor("s_at", [128, 4096], BF16)
    s_ln = nc.alloc_sbuf_tensor("s_ln", [128, 2048], BF16)
    s_dmp = nc.alloc_sbuf_tensor("s_dmp", [128, 2048], BF16)
    s_y2 = nc.alloc_sbuf_tensor("s_y2", [128, 2048], F32)
    s_wb = nc.alloc_sbuf_tensor("s_wb", [128, 4096], BF16)
    s_wd = nc.alloc_sbuf_tensor("s_wd", [128, 6144], BF16)
    s_cst = nc.alloc_sbuf_tensor("s_cst", [128, 448], BF16)
    s_bq = nc.alloc_sbuf_tensor("s_bq", [128, ND], F32)
    s_bk = nc.alloc_sbuf_tensor("s_bk", [128, ND], F32)
    s_bv = nc.alloc_sbuf_tensor("s_bv", [128, ND], F32)
    s_bo = nc.alloc_sbuf_tensor("s_bo", [128, ND], F32)
    s_Bw = nc.alloc_sbuf_tensor("s_Bw", [1, 128], F32)
    s_nB = nc.alloc_sbuf_tensor("s_nB", [128, ND], F32)
    s_st = nc.alloc_sbuf_tensor("s_st", [128, 128], F32)
    s_sx = nc.alloc_sbuf_tensor("s_sx", [128, NJ], F32)

    pS = nc.alloc_psum_tensor("pS", [128, 2048], F32)
    pO = nc.alloc_psum_tensor("pO", [128, 2048], F32)

    P = Prog(nc)
    op = P.op

    IDT = s_cst[:, 0:128]
    LMSK = s_cst[:, 128:256]
    NEGI = s_cst[:, 256:384]
    ONES = s_cst[:, 384:448]

    Q0, K0, V0 = 0, ND * S, 2 * ND * S

    def q_fm(c):
        return s_b[:, Q0 + S * c: Q0 + S * (c + 1)]

    def k_fm(c):
        return s_b[:, K0 + S * c: K0 + S * (c + 1)]

    def xfm(c):
        return s_f[:, S * c: S * (c + 1)]

    def ofm(c):
        return s_b[:, Q0 + S * c: Q0 + S * (c + 1)]

    def acc_view(i):
        return pS[:, 512 * i:512 * (i + 1)]

    def psb(tensor, lo, hi):
        base = 0 if tensor is pS else 4
        return ["ps%d" % b for b in
                range(base + lo // 512, base + (hi - 1) // 512 + 1)]

    ALLXNT = ["xnT%d" % j for j in range(NJ)]
    ALLYNT = ["ynT%d" % j for j in range(NJ)]
    ALLOH = ["o_h%d_%d" % (h, b) for h in range(NH) for b in range(2)]

    XQ = ("qx", "qx2", "qx3", "qx4")

    def x_chunk_dma(j):
        src = dX[128 * j:128 * (j + 1), :]
        dst = s_x[:, 1024 * j:1024 * (j + 1)]
        op("pool", lambda e, src=src, dst=dst: e.dma_start(dst, src),
           writes=["x%d" % j], queue=XQ[j % 4])

    def x_dma(t):
        x_chunk_dma(2 * t)
        x_chunk_dma(2 * t + 1)

    pre_dma = [None]

    def w_chunk_dma_pre():
        src0 = dWq.rearrange("(kc p) n -> p kc n", p=128)[:, :, 0:128]
        dstv = s_wb[:, 0:1024].rearrange("p (kc n) -> p kc n", n=128)
        op("pool", lambda e, src0=src0, dstv=dstv: e.dma_start(dstv, src0),
           writes=["wb0"], queue="qpool")

    w_chunk_dma_pre()
    x_dma(0)
    x_dma(1)
    op("act", lambda e: e.dma_start(s_cst[:], dCst), writes=["cst"], queue="qact")
    op("act", lambda e: e.dma_start(s_bq[:], dBq), writes=["bq"], queue="qact")
    op("act", lambda e: e.dma_start(s_bk[:], dBk), writes=["bk"], queue="qact")
    op("act", lambda e: e.dma_start(s_bv[:], dBv), writes=["bv"], queue="qact")
    op("act", lambda e: e.dma_start(s_bo[:], dBo), writes=["bo"], queue="qact")

    def emit_ln(src_ap, src_res, dst_ap, dst_res, pid=0, eng="dve",
                dst_war=(), war0=(), m_cell=None, m_res=None, m_out=None):
        b = 16 * pid

        def cell(i):
            return s_st[:, b + i:b + i + 1]

        r = lambda n: "st%d_%s" % (pid, n)
        dmp = s_dmp[:, 0:1024]
        if m_cell is None:
            if m_out is not None:
                mc, mres = m_out
            else:
                mc, mres = cell(2), r("m")
            op("dve", lambda e, mc=mc: e.tensor_scalar(
                dmp, src_ap, 1.0 / D, 0.0, ALU.mult, ALU.add,
                accum_out=mc),
               reads=[src_res], writes=[mres, "dmp"], war=war0)
        else:
            mc, mres = m_cell, m_res
        op("dve", lambda e: e.scalar_tensor_tensor(
            dmp, src_ap, 1.0 / D, src_ap, ALU.mult, ALU.mult,
            accum_out=cell(3)),
           reads=[src_res], writes=[r("e2"), "dmp"],
           war=(war0 if m_cell is not None else ()))
        op("dve", lambda e, mc=mc: e.tensor_scalar(cell(4), mc, mc, -EPS,
                                                   ALU.mult, ALU.add),
           reads=[mres], writes=[r("v")])
        op("dve", lambda e: e.tensor_tensor(s_st[:, b + 4:b + 5], cell(3),
                                            cell(4), ALU.subtract),
           reads=[r("e2"), r("v")], writes=[r("v")])
        op("act", lambda e: e.activation(cell(5), cell(4), AF.Ln,
                                         bias=0.0, scale=1.0),
           reads=[r("v")], writes=[r("l")])
        op("act", lambda e: e.activation(cell(5), cell(5), AF.Exp,
                                         bias=0.0, scale=-0.5),
           reads=[r("l")], writes=[r("rs")])
        op("dve", lambda e, mc=mc: e.tensor_scalar(cell(6), mc, cell(5), -1.0,
                                                   ALU.mult, ALU.mult),
           reads=[mres, r("rs")], writes=[r("n")])
        op("act", lambda e: e.activation(dst_ap, src_ap, AF.Identity,
                                         bias=cell(6), scale=cell(5)),
           reads=[src_res, r("rs"), r("n")], writes=[dst_res], war=dst_war)

    def p1_L(j):
        par = j % 2
        xs = s_x[:, D * j:D * (j + 1)]
        xn = s_ln[:, 1024 * par:1024 * (par + 1)]
        emit_ln(xs, "x%d" % j, xn, "lnr%d" % par, pid=j % 4, eng="dve",
                m_out=(s_sx[:, j:j + 1], "mx%d" % j))

    def p1_T(j):
        par = j % 2
        xn = s_ln[:, 1024 * par:1024 * (par + 1)]
        pb = pO[:, 512 * par:512 * (par + 1)].bitcast(BF16)
        bank = ["ps%d" % (4 + par)]
        for c in range(ND):
            op("pe",
               lambda e, c=c, xn=xn, pb=pb: e.transpose(
                   pb[:, 128 * c:128 * (c + 1)],
                   xn[:, 128 * c:128 * (c + 1)], IDT),
               reads=["lnr%d" % par, "cst"],
               writes=(bank if c == 0 else []),
               bump=(bank if c > 0 else []))
        dst = s_f[:].rearrange("p (c n) -> p c n", c=ND)[:, :, 128 * j:128 * (j + 1)]
        src = pb[:, 0:1024].rearrange("p (c n) -> p c n", c=ND)
        op("act", lambda e, dst=dst, src=src: e.copy(dst, src),
           reads=bank, writes=["xnT%d" % j])


    def wslot(slot):
        if slot < 4:
            return s_wb[:, 1024 * slot:1024 * (slot + 1)]
        return s_wd[:, 1024 * (slot - 4):1024 * (slot - 3)]

    def w_chunk_dma(dram, c, slot, queue="qpool"):
        src = dram.rearrange("(kc p) n -> p kc n", p=128)[:, :, 128 * c:128 * (c + 1)]
        dstv = wslot(slot).rearrange("p (kc n) -> p kc n", n=128)
        op("pool", lambda e, src=src, dstv=dstv: e.dma_start(dstv, src),
           writes=["wb%d" % slot], queue=queue)

    def fm_matmuls(slot, wres_list, reads_fn, out_view_fn, psum_res_fn,
                   src_fn=None):
        if src_fn is None:
            src_fn = xfm
        for si in range(4):
            for kc in range(ND):
                first = kc == 0
                op("pe",
                   lambda e, kc=kc, si=si, slot=slot, first=first:
                   e.matmul(out_view_fn(si),
                            wslot(slot)[:, 128 * kc:128 * (kc + 1)],
                            src_fn(kc)[:, 512 * si:512 * (si + 1)],
                            start=first, stop=(kc == ND - 1)),
                   reads=wres_list + reads_fn(si),
                   writes=([psum_res_fn(si)] if first else []),
                   bump=([] if first else [psum_res_fn(si)]))

    norm_pend = []

    def norm_sq(which, c, si, src_ap, dres):
        par = q2par[0]
        q2par[0] ^= 1
        q2 = s_wd[:, 2048 + 512 * par:2048 + 512 * (par + 1)]
        op("pool", lambda e, src_ap=src_ap, q2=q2: e.tensor_mul(
            q2, src_ap, src_ap),
           reads=[dres], writes=["q2%d" % par])
        bcol = (0 if which == "q" else 32) + 4 * c + si
        norm_pend.append((par, bcol))

    def norm_flush(n=1):
        for _ in range(n):
            if not norm_pend:
                return
            par, bcol = norm_pend.pop(0)
            nrm = pS[64 * par:64 * par + 1, 512:1024]
            q2 = s_wd[:, 2048 + 512 * par:2048 + 512 * (par + 1)]
            op("pe", lambda e, q2=q2, nrm=nrm, par=par: e.matmul(
                nrm, ONES[:, 0:1], q2, start=True, stop=True,
                tile_position=(0, 64 * par), skip_group_check=True),
               reads=["q2%d" % par, "cst"], writes=["ps1"])
            op("dve", lambda e, bcol=bcol, nrm=nrm: e.reduce_max(
                s_Bw[0:1, bcol:bcol + 1], nrm, axis=AX.X),
               reads=["ps1"], writes=["Bw%d" % bcol])

    ubank = [0]
    q2par = [0]

    fifo = []

    def fifo_step(k=1):
        while fifo and k > 0:
            fifo.pop(0)()
            k -= 1

    def fifo_drain():
        while fifo:
            fifo.pop(0)()

    def push_qk_unit(which, c, si):
        bias_t, bias_res, dest_fn = (
            (s_bq, "bq", q_fm) if which == "q" else (s_bk, "bk", k_fm))
        slot = (3 * c + (0 if which == "q" else 1)) % 6
        out = pS[:, 0:512]

        def piece(kc0):
            if kc0 == 0 and len(norm_pend) >= 2:
                norm_flush(1)
            for kc in (kc0, kc0 + 1):
                first = kc == 0
                op("pe",
                   lambda e, kc=kc, si=si, slot=slot, first=first, out=out:
                   e.matmul(out, wslot(slot)[:, 128 * kc:128 * (kc + 1)],
                            xfm(kc)[:, 512 * si:512 * (si + 1)],
                            start=first, stop=(kc == ND - 1),
                            skip_group_check=True),
                   reads=["wb%d" % slot] + ["xnT%d" % (4 * si + k)
                                            for k in range(4)],
                   writes=(["ps0"] if first else []),
                   bump=([] if first else ["ps0"]))
            if kc0 == ND - 2:
                dres = "%s_%d_%d" % (which, c, si)
                dst = dest_fn(c)[:, 512 * si:512 * (si + 1)]
                op("dve",
                   lambda e, dst=dst, out=out, bias_t=bias_t, c=c:
                   e.tensor_scalar(dst, out, bias_t[:, c:c + 1], None,
                                   ALU.add),
                   reads=["ps0", bias_res], writes=[dres])
                norm_sq(which, c, si, dst, dres)
        for kc0 in range(0, ND, 2):
            fifo.append(lambda kc0=kc0: piece(kc0))

    def emit_qk_unit(which, c, si):
        push_qk_unit(which, c, si)
        fifo_drain()

    def emit_B_calc(c):
        op("dve", lambda e, c=c: e.reduce_max(
            s_Bw[0:1, 64 + c:65 + c], s_Bw[0:1, 4 * c:4 * c + 4], axis=AX.X),
           reads=["Bw%d" % (4 * c + i) for i in range(4)], writes=["qm%d" % c])
        op("dve", lambda e, c=c: e.reduce_max(
            s_Bw[0:1, 72 + c:73 + c], s_Bw[0:1, 32 + 4 * c:36 + 4 * c],
            axis=AX.X),
           reads=["Bw%d" % (32 + 4 * c + i) for i in range(4)],
           writes=["km%d" % c])
        op("dve", lambda e, c=c: e.tensor_tensor(
            s_Bw[0:1, 80 + c:81 + c], s_Bw[0:1, 64 + c:65 + c],
            s_Bw[0:1, 72 + c:73 + c], ALU.mult),
           reads=["qm%d" % c, "km%d" % c], writes=["Bqk%d" % c])
        op("act", lambda e, c=c: e.activation(
            s_Bw[0:1, 88 + c:89 + c], s_Bw[0:1, 80 + c:81 + c], AF.Ln,
            bias=0.0, scale=1.0),
           reads=["Bqk%d" % c], writes=["Bln%d" % c])
        op("act", lambda e, c=c: e.activation(
            s_Bw[0:1, 96 + c:97 + c], s_Bw[0:1, 88 + c:89 + c], AF.Exp,
            bias=0.0, scale=0.5),
           reads=["Bln%d" % c], writes=["Bex%d" % c])
        nbrow = s_wd[0:1, 3072 + c:3073 + c]
        op("dve", lambda e, c=c, nbrow=nbrow: e.tensor_scalar(
            nbrow, s_Bw[0:1, 96 + c:97 + c], -0.125, None, ALU.mult),
           reads=["Bex%d" % c], writes=["nbrow%d" % c])

    def emit_B_nbp(c):
        nbrow = s_wd[0:1, 3072 + c:3073 + c]
        nbp = pS[:, 0:1]
        i1 = op("pe", lambda e, nbrow=nbrow, nbp=nbp: e.matmul(
            nbp[0:64, :], ONES[0:1, 0:64], nbrow, start=True, stop=True,
            skip_group_check=True),
            reads=["nbrow%d" % c, "cst"], writes=["ps0"])
        op("pe", lambda e, nbrow=nbrow, nbp=nbp: e.matmul(
            nbp[64:128, :], ONES[0:1, 0:64], nbrow, start=True, stop=True,
            tile_position=(0, 64), skip_group_check=True),
           reads=["nbrow%d" % c, "cst"], deps=[i1], bump=["ps0"])
        op("dve", lambda e, c=c, nbp=nbp: e.tensor_copy(
            s_nB[:, c:c + 1], nbp),
           reads=["ps0"], writes=["nB%d" % c])

    def emit_v_round(c, si):
        slot = (3 * c + 2) % 6
        bank = ubank[0]
        ubank[0] ^= 1
        bres = "ps%d" % bank
        out = pS[:, 512 * bank:512 * (bank + 1)]
        for kc in range(ND):
            first = kc == 0
            op("pe",
               lambda e, kc=kc, si=si, slot=slot, first=first, out=out:
               e.matmul(out, wslot(slot)[:, 128 * kc:128 * (kc + 1)],
                        xfm(kc)[:, 512 * si:512 * (si + 1)],
                        start=first, stop=(kc == ND - 1)),
               reads=["wb%d" % slot] + ["xnT%d" % (4 * si + k) for k in range(4)],
               writes=([bres] if first else []),
               bump=([] if first else [bres]))
        vstage = s_at[:, 2048 * (c % 2):2048 * (c % 2) + 2048]
        atres = "at%d" % (c % 2)
        op("dve",
           lambda e, si=si, c=c, vstage=vstage, out=out: e.tensor_scalar(
               vstage[:, 512 * si:512 * (si + 1)], out,
               s_bv[:, c:c + 1], None, ALU.add),
           reads=[bres, "bv"],
           writes=([atres] if si == 0 else []),
           bump=([] if si == 0 else [atres]))
        pb = out.bitcast(BF16)
        for jj in range(4):
            j = 4 * si + jj
            op("pe",
               lambda e, j=j, jj=jj, pb=pb, vstage=vstage: e.transpose(
                   pb[:, 128 * jj:128 * (jj + 1)],
                   vstage[:, 128 * j:128 * (j + 1)], IDT),
               reads=[atres, "cst"],
               writes=([bres] if jj == 0 else []),
               bump=([] if jj == 0 else [bres]))
        for jj in range(4):
            j = 4 * si + jj
            for hh in range(2):
                h = 2 * c + hh
                dst = s_b[:, V0 + 1040 * j + 65 * h:
                          V0 + 1040 * j + 65 * h + 64]
                op("dve",
                   lambda e, dst=dst, hh=hh, jj=jj, pb=pb: e.tensor_copy(
                       dst, pb[:, 128 * jj + 64 * hh:128 * jj + 64 * hh + 64]),
                   reads=[bres, "vones"],
                   writes=(["vx_%d_%d" % (j, c)] if hh == 0 else []),
                   bump=([] if hh == 0 else ["vx_%d_%d" % (j, c)]))

    op("pool", lambda e: e.memset(s_b[:, V0:V0 + 1040 * NJ + 128], 1.0),
       writes=["vones"])

    def emit_norm_half(h, hf, part):
        c = h // 2
        lo = 1024 * hf
        bks = ["ps%d" % (4 + 2 * hf), "ps%d" % (5 + 2 * hf)]
        o_res = "o_h%d_%d" % (h, hf)
        osc = s_ln[0:64, lo:lo + 1024]
        rr = s_dmp[64:65, lo:lo + 1024]
        if part == 0:
            op("dve", lambda e, osc=osc, lo=lo: e.tensor_copy(
                osc, pO[0:64, lo:lo + 1024]),
               reads=bks, writes=["osc%d" % hf],
               war=(["lnr0", "lnr1"] if h == 0 else ()))
            op("act", lambda e, lo=lo: e.activation(
                pO[64:65, lo:lo + 1024], pO[64:65, lo:lo + 1024], AF.Ln,
                bias=0.0, scale=1.0),
               reads=[bks[0]], writes=bks)
            op("act", lambda e, rr=rr, lo=lo: e.activation(
                rr, pO[64:65, lo:lo + 1024], AF.Exp, bias=0.0, scale=-1.0),
               reads=bks, writes=["rrq%d" % hf], bump=["dmp"],
               war=(["dmp"] if h == 0 else ()))
        else:
            i1 = op("pe", lambda e, rr=rr, lo=lo: e.matmul(
                pO[0:64, lo:lo + 512], s_cst[64:65, 384:448], rr[:, 0:512],
                start=True, stop=True, skip_group_check=True),
                reads=["rrq%d" % hf, "osc%d" % hf], writes=[bks[0]])
            op("pe", lambda e, rr=rr, lo=lo: e.matmul(
                pO[0:64, lo + 512:lo + 1024], s_cst[64:65, 384:448],
                rr[:, 512:1024],
                start=True, stop=True, skip_group_check=True),
                reads=["rrq%d" % hf, "osc%d" % hf], writes=[bks[1]])
            qres2 = ["q_%d_%d" % (c, 2 * hf), "q_%d_%d" % (c, 2 * hf + 1)]
            if h % 2 == 0:
                dst = ofm(c)[0:64, lo:lo + 1024]
                op("dve", lambda e, dst=dst, osc=osc, lo=lo: e.tensor_tensor(
                    dst, osc, pO[0:64, lo:lo + 1024], ALU.mult),
                   reads=["osc%d" % hf] + bks, writes=[o_res],
                   war=qres2, bump=qres2)
            else:
                onorm = s_dmp[0:64, lo:lo + 1024]
                op("dve", lambda e, onorm=onorm, osc=osc, lo=lo:
                   e.tensor_tensor(onorm, osc, pO[0:64, lo:lo + 1024],
                                   ALU.mult),
                   reads=["osc%d" % hf] + bks, writes=["onq%d" % hf],
                   bump=["dmp"],
                   war=(["dmp"] if h == 1 else ()))
                op("sp", lambda e, c=c, onorm=onorm, lo=lo: e.dma_start(
                    ofm(c)[64:128, lo:lo + 1024], onorm),
                   reads=["onq%d" % hf], writes=[o_res], queue="qsp",
                   war=qres2)

    kpadv = [s_y2[:, 0:1024].bitcast(BF16), s_y2[:, 1024:2048].bitcast(BF16)]
    op("dve", lambda e: e.memset(kpadv[0][64:128, :], 0.0), writes=["kpz0"])
    op("dve", lambda e: e.memset(kpadv[1][0:64, :], 0.0), writes=["kpz1"])
    pend_av = []
    pend_n1 = [None]
    sctr = [1]

    def emit_av_chunk(ph, pk, q0, n):
        c = ph // 2
        par = pk % 2
        atres = "at%d" % par
        first = pk == 0
        lastk = min((q0 + n - 1) // 128, NJ - 1)
        banks = psb(pO, q0, q0 + n)
        op("pe",
           lambda e, j=pk, h=ph, q0=q0, n=n, par=par, first=first,
           stopf=(pk == lastk):
           e.matmul(pO[:, q0:q0 + n],
                    s_b[:, V0 + 1040 * j + 65 * h:
                        V0 + 1040 * j + 65 * h + 128],
                    s_at[:, 2048 * par + (q0 - 128 * j):
                         2048 * par + (q0 - 128 * j) + n],
                    start=first, stop=stopf, skip_group_check=True),
           reads=[atres, "vx_%d_%d" % (pk, c), "vones"],
           writes=(banks if first else []),
           bump=([] if first else banks))

    def emit_pend_batch(pv):
        ph, pk, pch = pv
        for (q0, n) in pch:
            emit_av_chunk(ph, pk, q0, n)

        def post():
            if pk == NJ - 1:
                emit_norm_half(ph, 1, 0)
                pend_n1[0] = ph
            elif pk == 7:
                emit_norm_half(ph, 0, 0)
                emit_norm_half(ph, 0, 1)
        return post

    def emit_attn_step(h, ktc):
        c = h // 2
        hp = h % 2
        if ktc == 8 and 1 <= h < NH - 1:
            emit_kpad_copy(h + 1)
        post = None
        if len(pend_av) == 2:
            if pend_av[0][1] == 0 and pend_n1[0] is not None:
                emit_norm_half(pend_n1[0], 1, 1)
                pend_n1[0] = None
            post = emit_pend_batch(pend_av.pop(0))
        defer_n1 = pend_n1[0] is not None
        qres = ["q_%d_%d" % (c, si) for si in range(4)]
        par = ktc % 2
        atres = "at%d" % par
        chunks = []
        q0 = 128 * ktc
        while q0 < S:
            n = min(512 - (q0 % 512), S - q0)
            chunks.append((q0, n))
            q0 += n
        first_exp = True
        gi = 0
        while gi < len(chunks):
            lo = 0 if not fifo else 1
            s = sctr[0]
            if s < lo:
                s = lo
            g = 2 if (gi + 1 < len(chunks)
                      and chunks[gi + 1][1] == chunks[gi][1]
                      and s != 3) else 1
            sctr[0] = lo + (s + g - lo) % (4 - lo)
            for u in range(g):
                q0, n = chunks[gi + u]
                sl = s + u
                diag = gi + u == 0
                op("pe",
                   lambda e, sl=sl, q0=q0, n=n, ktc=ktc, hp=hp, c=c,
                   diag=diag:
                   e.matmul(pS[:, 512 * sl:512 * sl + n],
                            kpadv[hp][:, 128 * ktc:128 * (ktc + 1)],
                            q_fm(c)[:, q0:q0 + n],
                            start=True, stop=(not diag),
                            skip_group_check=True),
                   reads=["kp%d" % hp] + qres, writes=["ps%d" % sl])
                if diag:
                    op("pe",
                       lambda e, sl=sl: e.matmul(
                           pS[:, 512 * sl:512 * sl + 128], NEGI, LMSK,
                           start=False, stop=True, skip_group_check=True),
                       reads=["cst"], bump=["ps%d" % sl])
            q0, n = chunks[gi]
            dof = 2048 * par + (q0 - 128 * ktc)
            if g == 2:
                src = pS[:, 512 * s:512 * (s + 2)].rearrange(
                    "p (g n) -> p g n", g=2)[:, :, 0:n]
                dst = s_at[:, dof:dof + 2 * n].rearrange(
                    "p (g n) -> p g n", g=2)
            else:
                src = pS[:, 512 * s:512 * s + n]
                dst = s_at[:, dof:dof + n]
            op("act",
               lambda e, src=src, dst=dst, c=c: e.activation(
                   dst, src, AF.Exp, bias=s_nB[:, c:c + 1], scale=QSCALE),
               reads=["nB%d" % c] + ["ps%d" % (s + k) for k in range(g)],
               writes=([atres] if first_exp else []),
               bump=([] if first_exp else [atres]))
            first_exp = False
            fifo_step(4 if defer_n1 else 1)
            gi += g
        if defer_n1:
            emit_norm_half(pend_n1[0], 1, 1)
            pend_n1[0] = None
        if post is not None:
            post()
        pend_av.append((h, ktc, chunks))

    def emit_kpad_copy(h):
        c = h // 2
        rbase = 64 * (h % 2)
        hp = h % 2
        kres = ["k_%d_%d" % (c, si) for si in range(4)]
        op("dve",
           lambda e, hp=hp, c=c, rbase=rbase: e.tensor_copy(
               kpadv[hp][rbase:rbase + 64, :], k_fm(c)[rbase:rbase + 64, :]),
           reads=kres + ["kpz%d" % hp], writes=["kp%d" % hp])

    def attention_drain():
        while pend_av:
            if pend_n1[0] is not None:
                emit_norm_half(pend_n1[0], 1, 1)
                pend_n1[0] = None
            emit_pend_batch(pend_av.pop(0))()

    for t in range(NJ + 1):
        if t < NJ:
            if t % 2 == 0 and t // 2 + 2 < NJ // 2:
                x_dma(t // 2 + 2)
            p1_L(t)
            if t == 0:
                w_chunk_dma(dWk, 0, 1)
                w_chunk_dma(dWv, 0, 2)
            elif t == 2:
                w_chunk_dma(dWq, 1, 3)
            elif t == 4:
                w_chunk_dma(dWk, 1, 4)
            elif t == 6:
                w_chunk_dma(dWv, 1, 5)
        if t >= 1:
            j = t - 1
            p1_T(j)
            if j % 4 == 3:
                si = j // 4
                emit_qk_unit("q", 0, si)
                emit_qk_unit("k", 0, si)
                emit_v_round(0, si)
    norm_flush(99)
    emit_B_calc(0)
    emit_B_nbp(0)
    emit_kpad_copy(0)
    emit_kpad_copy(1)
    w_chunk_dma(dWq, 2, 0)
    w_chunk_dma(dWk, 2, 1)
    w_chunk_dma(dWv, 2, 2)
    if trunc_step is not None and trunc_step <= 0:
        P.emit_all(final_deps=[])
        return nc

    for c in range(1, ND):
        steps = [(h, ktc) for h in (2 * c - 2, 2 * c - 1) for ktc in range(NJ)]
        for w, si in [(w, si) for si in range(4) for w in ("q", "k")]:
            push_qk_unit(w, c, si)
        fifo.append(lambda c=c: (norm_flush(99), emit_B_calc(c)))
        fifo.append(lambda c=c: emit_B_nbp(c))
        for h, ktc in steps:
            emit_attn_step(h, ktc)
        fifo_drain()
        attention_drain()
        for si in range(4):
            emit_v_round(c, si)
        if c + 2 < ND:
            w_chunk_dma(dWq, c + 2, (3 * (c + 2)) % 6)
            w_chunk_dma(dWk, c + 2, (3 * (c + 2) + 1) % 6)
            w_chunk_dma(dWv, c + 2, (3 * (c + 2) + 2) % 6)
        if trunc_step is not None and c >= trunc_step:
            P.emit_all(final_deps=[])
            return nc
    for h in (NH - 2, NH - 1):
        for ktc in range(NJ):
            emit_attn_step(h, ktc)
    attention_drain()
    emit_norm_half(NH - 1, 1, 1)
    pend_n1[0] = None
    if trunc:
        P.emit_all(final_deps=[])
        return nc

    for c in range(ND):
        slot = c % 2
        w_chunk_dma(dWo, c, slot)
        fm_matmuls(slot, ["wb%d" % slot],
                   lambda si: ["o_h%d_%d" % (h2, si // 2) for h2 in range(NH)],
                   acc_view, lambda si: "ps%d" % si, src_fn=ofm)
        for si in range(4):
            op("act",
               lambda e, c=c, si=si: e.activation(
                   s_f[:, S * c + 512 * si: S * c + 512 * (si + 1)],
                   acc_view(si), AF.Identity, bias=s_bo[:, c:c + 1], scale=1.0),
               reads=["ps%d" % si, "bo"], writes=["zf_%d_%d" % (c, si)],
               war=["xnT%d" % (4 * si + k) for k in range(4)])

    def p3_A(j):
        par = j % 2
        bank1 = "ps%d" % (4 + par)
        pb = pO[:, 512 * par:512 * (par + 1)].bitcast(BF16)
        for c in range(ND):
            op("pe",
               lambda e, c=c, j=j, pb=pb: e.transpose(
                   pb[:, 128 * c:128 * (c + 1)],
                   s_f[:, S * c + 128 * j: S * c + 128 * (j + 1)], IDT),
               reads=["zf_%d_%d" % (c, j // 4), "cst"],
               writes=([bank1] if c == 0 else []),
               bump=([] if c == 0 else [bank1]))
        ztm = s_ln[:, 1024 * par:1024 * (par + 1)]
        op("dve", lambda e, pb=pb, ztm=ztm: e.tensor_copy(ztm, pb[:, 0:1024]),
           reads=[bank1], writes=["lnr%d" % par],
           war=["osc0", "osc1"])
        yn = s_at[:, 1024 * par:1024 * (par + 1)]
        emit_ln(ztm, "lnr%d" % par, yn, "yn%d" % par, pid=j % 4, eng="dve",
                dst_war=("at0",) if j < 2 else (),
                war0=(["rrq0", "rrq1", "onq0", "onq1"] if j == 0 else ()))

    def p3_B(j):
        par = j % 2
        yn = s_at[:, 1024 * par:1024 * (par + 1)]
        op("pool",
           lambda e, j=j, yn=yn: e.tensor_tensor(
               s_x[:, D * j:D * (j + 1)], s_x[:, D * j:D * (j + 1)], yn,
               ALU.add),
           reads=["yn%d" % par, "x%d" % j], writes=["x%d" % j])

    def p3_C(j):
        par = j % 2
        yn2 = s_at[:, 2048 + 1024 * par:2048 + 1024 * (par + 1)]
        emit_ln(s_x[:, D * j:D * (j + 1)], "x%d" % j, yn2, "yn2%d" % par,
                pid=4 + j % 4, eng="dve",
                m_cell=s_sx[:, j:j + 1], m_res="mx%d" % j,
                dst_war=("at1",) if j < 2 else ())

    def p3_D(j):
        par = j % 2
        yn2 = s_at[:, 2048 + 1024 * par:2048 + 1024 * (par + 1)]
        bank2 = "ps%d" % (6 + par)
        pb2 = pO[:, 1024 + 512 * par:1536 + 512 * par].bitcast(BF16)
        for c in range(ND):
            op("pe",
               lambda e, c=c, pb2=pb2, yn2=yn2: e.transpose(
                   pb2[:, 128 * c:128 * (c + 1)],
                   yn2[:, 128 * c:128 * (c + 1)], IDT),
               reads=["yn2%d" % par, "cst"],
               writes=([bank2] if c == 0 else []),
               bump=([] if c == 0 else [bank2]))
        dst = s_f[:].rearrange("p (c n) -> p c n", c=ND)[:, :, 128 * j:128 * (j + 1)]
        src = pb2[:, 0:1024].rearrange("p (c n) -> p c n", c=ND)
        op("act", lambda e, dst=dst, src=src: e.copy(dst, src),
           reads=[bank2], writes=["ynT%d" % j],
           war=["zf_%d_%d" % (c2, j // 4) for c2 in range(ND)])

    for t in range(NJ + 3):
        if t < NJ:
            p3_A(t)
        if 0 <= t - 1 < NJ:
            p3_B(t - 1)
        if 0 <= t - 2 < NJ:
            p3_C(t - 2)
        if 0 <= t - 3 < NJ:
            p3_D(t - 3)

    for hc in range(NHC):
        gslot = hc % 2
        uslot = 2 + hc % 2
        w_chunk_dma(dWg, hc, gslot)
        w_chunk_dma(dWu, hc, uslot)
        hwar = []
        if hc < ND:
            hwar = ["zf_%d_%d" % (hc, s2) for s2 in range(4)]
        elif hc < 2 * ND:
            hwar = ["k_%d_%d" % (hc - ND, s2) for s2 in range(4)]
        elif hc == 2 * ND:
            hwar = (["vx_%d_%d" % (j2, c2) for j2 in range(NJ)
                     for c2 in range(ND)] + ["vones"])
        for si in range(4):
            for kc in range(ND):
                first = kc == 0
                op("pe",
                   lambda e, kc=kc, si=si, gslot=gslot, first=first:
                   e.matmul(acc_view(si),
                            wslot(gslot)[:, 128 * kc:128 * (kc + 1)],
                            xfm(kc)[:, 512 * si:512 * (si + 1)],
                            start=first, stop=(kc == ND - 1)),
                   reads=["wb%d" % gslot] + ["ynT%d" % (4 * si + k) for k in range(4)],
                   writes=(["ps%d" % si] if first else []),
                   bump=([] if first else ["ps%d" % si]))
            for kc in range(ND):
                first = kc == 0
                op("pe",
                   lambda e, kc=kc, si=si, uslot=uslot, first=first:
                   e.matmul(pO[:, 512 * si:512 * (si + 1)],
                            wslot(uslot)[:, 128 * kc:128 * (kc + 1)],
                            xfm(kc)[:, 512 * si:512 * (si + 1)],
                            start=first, stop=(kc == ND - 1)),
                   reads=["wb%d" % uslot] + ["ynT%d" % (4 * si + k) for k in range(4)],
                   writes=(["ps%d" % (4 + si)] if first else []),
                   bump=([] if first else ["ps%d" % (4 + si)]))
            stg = s_dmp[:, 1024 + 512 * (si % 2): 1536 + 512 * (si % 2)]
            op("act",
               lambda e, si=si, stg=stg: e.activation(
                   stg, acc_view(si), AF.Silu, bias=0.0, scale=1.0 / 127.0),
               reads=["ps%d" % si], writes=["silu%d" % (si % 2)],
               war=(("rrq1", "onq1") if (hc == 0 and si < 2) else ()))
            op("dve",
               lambda e, hc=hc, si=si, stg=stg: e.tensor_tensor(
                   s_b[:, S * hc + 512 * si: S * hc + 512 * (si + 1)],
                   stg, pO[:, 512 * si:512 * (si + 1)], ALU.mult),
               reads=["silu%d" % (si % 2), "ps%d" % (4 + si)],
               writes=["h_%d_%d" % (hc, si)],
               war=hwar if si == 0 else ())

    def wd_dma(c):
        slot = c % 2
        src = dWd.rearrange("(hc p) n -> p hc n", p=128)[:, :, 128 * c:128 * (c + 1)]
        dstv = s_wd[:, 3072 * slot:3072 * (slot + 1)].rearrange(
            "p (hc n) -> p hc n", n=128)
        op("pool", lambda e, src=src, dstv=dstv: e.dma_start(dstv, src),
           writes=["wd%d_0" % slot, "wd%d_1" % slot], queue="qx",
           war=((["wb4", "wb5", "q20", "q21"]
                 + ["nbrow%d" % c2 for c2 in range(ND)]) if c < 2 else ()))

    wd_dma(0)
    wd_dma(1)
    for c in range(ND):
        slot = c % 2
        for si in range(4):
            for hc in range(NHC):
                first = hc == 0
                op("pe",
                   lambda e, hc=hc, si=si, slot=slot, first=first:
                   e.matmul(acc_view(si),
                            s_wd[:, 3072 * slot + 128 * hc:
                                 3072 * slot + 128 * (hc + 1)],
                            s_b[:, S * hc + 512 * si: S * hc + 512 * (si + 1)],
                            start=first, stop=(hc == NHC - 1)),
                   reads=["wd%d_%d" % (slot, hc // 12), "h_%d_%d" % (hc, si)],
                   writes=(["ps%d" % si] if first else []),
                   bump=([] if first else ["ps%d" % si]))
            op("act",
               lambda e, c=c, si=si: e.activation(
                   s_f[:, S * c + 512 * si: S * c + 512 * (si + 1)],
                   acc_view(si), AF.Identity, bias=0.0, scale=FFS),
               reads=["ps%d" % si], writes=["df_%d_%d" % (c, si)],
               war=(ALLYNT if (c == 0 and si == 0) else ()))
        if c + 2 < ND:
            wd_dma(c + 2)

    oY = []

    def p6_T(j):
        par = j % 2
        bank = "ps%d" % (4 + par)
        pb = pO[:, 512 * par:512 * (par + 1)].bitcast(BF16)
        for c in range(ND):
            op("pe",
               lambda e, c=c, j=j, pb=pb: e.transpose(
                   pb[:, 128 * c:128 * (c + 1)],
                   s_f[:, S * c + 128 * j: S * c + 128 * (j + 1)], IDT),
               reads=["df_%d_%d" % (c, j // 4), "cst"],
               writes=([bank] if c == 0 else []),
               bump=([] if c == 0 else [bank]))
        dtm = s_ln[:, 1024 * par:1024 * (par + 1)]
        op("act", lambda e, pb=pb, dtm=dtm: e.copy(dtm, pb[:, 0:1024]),
           reads=[bank], writes=["lnr%d" % par])

    def p6_L(j):
        par = j % 2
        dtm = s_ln[:, 1024 * par:1024 * (par + 1)]
        y2 = s_y2[:, 1024 * par:1024 * (par + 1)]
        emit_ln(dtm, "lnr%d" % par, y2, "y2_%d" % par, pid=j % 4, eng="dve",
                dst_war=("kp%d" % par, "kpz%d" % par) if j < 2 else ())

    def p6_AD(j):
        par = j % 2
        y2 = s_y2[:, 1024 * par:1024 * (par + 1)]
        y2r = "y2_%d" % par
        if j % 2:
            op("pool",
               lambda e, j=j, y2=y2: e.tensor_tensor(
                   y2, y2, s_x[:, D * j:D * (j + 1)], ALU.add),
               reads=[y2r, "x%d" % j], writes=[y2r])
        else:
            op("dve",
               lambda e, j=j, y2=y2: e.tensor_tensor(
                   y2, y2, s_x[:, D * j:D * (j + 1)], ALU.add),
               reads=[y2r, "x%d" % j], writes=[y2r])
        yq = ("qsp", "qsp2", "qsp3")[j % 3]
        oY.append(op("sp",
                     lambda e, j=j, y2=y2: e.dma_start(
                         dY[128 * j:128 * (j + 1), :], y2),
                     reads=[y2r], writes=["yout%d" % j], queue=yq))

    for t in range(NJ + 2):
        if t < NJ:
            p6_T(t)
        if 0 <= t - 1 < NJ:
            p6_L(t - 1)
        if 0 <= t - 2 < NJ:
            p6_AD(t - 2)

    P.emit_all(final_deps=oY)
    return nc


def kernel(L0_attn_X, L0_attn_Wq, L0_attn_bq, L0_attn_Wk, L0_attn_bk,
           L0_attn_Wv, L0_attn_bv, L0_attn_Wo, L0_attn_bo,
           L0_ffn_w_gate, L0_ffn_w_up, L0_ffn_w_down):
    X = np.ascontiguousarray(np.asarray(L0_attn_X, dtype=np.float32))
    wq = np.ascontiguousarray(np.asarray(L0_attn_Wq, dtype=np.float32))
    wk = np.ascontiguousarray(np.asarray(L0_attn_Wk, dtype=np.float32))
    wv = np.ascontiguousarray(np.asarray(L0_attn_Wv, dtype=np.float32))
    wo = np.ascontiguousarray(np.asarray(L0_attn_Wo, dtype=np.float32))
    bqr = np.ascontiguousarray(np.asarray(L0_attn_bq, np.float32).reshape(ND, 128).T)
    bkr = np.ascontiguousarray(np.asarray(L0_attn_bk, np.float32).reshape(ND, 128).T)
    bvr = np.ascontiguousarray(np.asarray(L0_attn_bv, np.float32).reshape(ND, 128).T)
    bor = np.ascontiguousarray(np.asarray(L0_attn_bo, np.float32).reshape(ND, 128).T)
    wg = np.ascontiguousarray(np.asarray(L0_ffn_w_gate, np.int32).T)
    wu = np.ascontiguousarray(np.asarray(L0_ffn_w_up, np.int32).T)
    wd = np.ascontiguousarray(np.asarray(L0_ffn_w_down, np.int32).T)

    consts = np.zeros((128, 448), dtype=ml_dtypes.bfloat16)
    consts[:, 0:128] = np.eye(128, dtype=np.float32).astype(ml_dtypes.bfloat16)
    consts[:, 128:256] = np.tril(np.ones((128, 128), np.float32), -1).astype(
        ml_dtypes.bfloat16)
    consts[:, 256:384] = (LNEG * np.eye(128, dtype=np.float32)).astype(
        ml_dtypes.bfloat16)
    consts[:, 384:448] = np.ones((128, 64), np.float32).astype(ml_dtypes.bfloat16)

    nc = build_program()
    shared = {"wq": wq, "wk": wk, "wv": wv, "wo": wo,
              "bqr": bqr, "bkr": bkr, "bvr": bvr, "bor": bor,
              "wg": wg, "wu": wu, "wd": wd, "consts": consts}
    in_maps = [dict(shared, x=np.ascontiguousarray(X[i])) for i in range(8)]
    res = run_bass_kernel_spmd(nc, in_maps, core_ids=list(range(8)))
    return np.stack([np.asarray(res.results[i]["y"], dtype=np.float32)
                     for i in range(8)], axis=0)
